# version 18 (final)
# baseline (speedup 1.0000x reference)
import contextlib
import numpy as np
import ml_dtypes
import concourse.bass as bass
import concourse.mybir as mybir
from concourse.bass import IndirectOffsetOnAxis
from concourse.bass_utils import run_bass_kernel_spmd

F32 = mybir.dt.float32
BF16 = mybir.dt.bfloat16
F16 = mybir.dt.float16
I32 = mybir.dt.int32
AF = mybir.ActivationFunctionType
ALU = mybir.AluOpType

S, D, H, DK = 2048, 1024, 4, 256
NCH, NT, E, KTOP, FF = 16, 16, 8, 512, 2048
XR = 1152
EPS_L2, EPS_RMS, EPS_LN = 1e-6, 1e-5, 1e-5
BIS_IT = 45
ENGS = ("T", "V", "SC", "SY", "GP")


class Ctx:
    def __init__(self, nc):
        self.nc = nc
        self.sem = {}
        self.cnt = dict.fromkeys(ENGS, 0)
        self.wmark = {}
        self.lastwait = {"SY": 0, "GP": 0}
        self.chained = {"SY": 0, "GP": 0}

    def inc(self, instr, eng, d=None):
        if d is None:
            d = 16 if eng in ("SY", "GP") else 1
        instr.then_inc(self.sem[eng], d)
        self.cnt[eng] += d
        return (eng, self.cnt[eng])

    def tok(self, eng):
        if eng in ("SY", "GP"):
            self.lastwait[eng] = max(self.lastwait[eng], self.cnt[eng])
        return (eng, self.cnt[eng])

    def sw(self, eng_obj, name):
        key = (name, name)
        if self.cnt[name] > self.wmark.get(key, 0):
            eng_obj.wait_ge(self.sem[name], self.cnt[name])
            self.wmark[key] = self.cnt[name]

    def dma(self, eng_obj, q, indirect=False, **kw):
        lw = self.lastwait[q]
        if self.chained[q] < lw:
            eng_obj.wait_ge(self.sem[q], lw)
            self.chained[q] = lw
        instr = eng_obj.indirect_dma_start(**kw) if indirect else eng_obj.dma_start(**kw)
        return self.inc(instr, q)

    def wait(self, eng_obj, waiter, tokens):
        if tokens and isinstance(tokens, tuple) and isinstance(tokens[0], str):
            tokens = [tokens]
        skey = (waiter, waiter)
        if self.cnt[waiter] > self.wmark.get(skey, 0):
            eng_obj.wait_ge(self.sem[waiter], self.cnt[waiter])
            self.wmark[skey] = self.cnt[waiter]
        for (e, v) in tokens:
            if e in ("SY", "GP"):
                self.lastwait[e] = max(self.lastwait[e], v)
            if v <= 0:
                continue
            if e == waiter and e not in ("SY", "GP"):
                continue
            key = (waiter, e)
            if self.wmark.get(key, 0) >= v:
                continue
            eng_obj.wait_ge(self.sem[e], v)
            self.wmark[key] = v

    def barrier_toks(self):
        return [(e, self.cnt[e]) for e in ENGS]


class Arena:
    def __init__(self, ar, n):
        self.ar, self.n, self.placed = ar, n, []

    def alloc(self, name, elems, phases, dt=BF16):
        sz = ((elems if dt == BF16 else 2 * elems) + 1) // 2 * 2
        cands = sorted([(o, s) for (_, o, s, p) in self.placed if p & set(phases)])
        off = 0
        for (o, s) in cands:
            if off + sz <= o:
                break
            off = max(off, o + s)
        assert off + sz <= self.n, (name, off, sz, self.n)
        self.placed.append((name, off, sz, set(phases)))
        if dt == BF16:
            return self.ar[:, off:off + elems]
        return self.ar.bitcast(F32)[:, off // 2:off // 2 + elems]


def build(dbg=(), upto="p5"):
    dbg = set(dbg)
    nc = bass.Bass("TRN2", target_bir_lowering=False, debug=False)
    g = {"nc": nc, "dbgset": dbg}

    def dram(name, shp, dt, kind):
        g[name] = nc.dram_tensor(name, shp, dt, kind=kind).ap()
    dram("hs", [S, D], F32, "ExternalInput")
    dram("wqT", [D, D], F32, "ExternalInput")
    dram("wkT", [D, D], F32, "ExternalInput")
    dram("wvT", [D, D], F32, "ExternalInput")
    dram("wsT", [D, 16], F32, "ExternalInput")
    dram("mixb", [1, 4], F32, "ExternalInput")
    dram("convw", [D, 12], F32, "ExternalInput")
    dram("onw", [1, DK], F32, "ExternalInput")
    dram("woT", [D, D], F32, "ExternalInput")
    dram("cwT", [D, E], F32, "ExternalInput")
    dram("w1", [E, D, FF], F32, "ExternalInput")
    dram("w2", [E, FF, D], F32, "ExternalInput")
    dram("out", [S, D], F32, "ExternalOutput")
    dram("hs16", [S, D], BF16, "Internal")
    dram("sc16", [3 * DK, S], BF16, "Internal")
    dram("ost16", [S, D], BF16, "Internal")
    dram("x1s", [S, XR], BF16, "Internal")
    dram("xes0", [KTOP, XR], BF16, "Internal")
    dram("xes1", [KTOP, XR], BF16, "Internal")
    dram("accd", [S, D], F32, "Internal")
    dram("w116", [E, D, FF], BF16, "Internal")
    dram("w216", [E, FF, D], BF16, "Internal")

    DBG = {}
    for name, shp in (("v", [256, S]), ("qn", [256, S]), ("kn", [256, S]),
                      ("bgm", [16, S]), ("ema", [256, S]), ("x1", [S, D]),
                      ("probs", [E, S]), ("thr", [1, 8]), ("idx", [128, 32]),
                      ("acc", [S, D]), ("tt0", [128, 128]), ("ost", [S, D])):
        if name in dbg:
            DBG[name] = nc.dram_tensor("dbg_" + name, shp, F32, kind="ExternalOutput").ap()
    g["DBG"] = DBG

    ii, jj = np.mgrid[0:128, 0:128]
    consts = [
        ("c_identb", np.eye(128, dtype=ml_dtypes.bfloat16)),
        ("c_identf", np.eye(128, dtype=np.float32)),
        ("c_maskSL", (ii > jj).astype(np.float32)),
        ("c_maskSU", (ii < jj).astype(np.float32)),
        ("c_maskSLn", -(ii > jj).astype(np.float32)),
        ("c_maskSUn", -(ii < jj).astype(np.float32)),
        ("c_maskIU", (ii <= jj).astype(np.float32)),
        ("c_ones_b", np.ones((128, 1), dtype=ml_dtypes.bfloat16)),
        ("c_ones_f", np.ones((128, 1), dtype=np.float32)),
        ("c_ones_h", np.ones((128, 1), dtype=np.float16)),
        ("c_onesr_f", np.ones((33, 128), dtype=np.float32)),
        ("c_iota_d", np.tile(np.arange(KTOP, dtype=np.float16)[None, :], (128, 1))),
        ("c_iota_tok", (np.arange(16)[None, :] * 128 + np.arange(128)[:, None]).astype(np.float16)),
        ("c_neg1", np.full((128, 16), -1.0, dtype=np.float16)),
        ("c_eps", np.tile(np.array([[EPS_L2, EPS_RMS, EPS_LN, 0.0]], dtype=np.float32), (128, 1))),
    ]
    g["cdram"] = {n: nc.inline_tensor(v, n + "_d").ap() for n, v in consts}
    g["cnames"] = [n for n, _ in consts]

    st = contextlib.ExitStack()
    sbuf = lambda n, shp, dt: st.enter_context(nc.sbuf_tensor(n, shp, dt))
    psum = lambda n, shp, dt: st.enter_context(nc.psum_tensor(n, shp, dt))

    ARN = 90112
    arT = sbuf("AR", [128, ARN], BF16)
    g["AR"] = arT
    ar = Arena(arT, ARN)
    hF = {"h"}
    g["hs_fm8"] = ar.alloc("hs_fm", 16384, {"p0", "p1"} | hF)
    g["hs_fm"] = [g["hs_fm8"][:, 2048 * f:2048 * (f + 1)] for f in range(8)]
    g["qf"] = ar.alloc("qf", 4096, hF)
    g["kf"] = ar.alloc("kf", 4096, hF)
    ktm_a = ar.alloc("ktm", 4096, hF)
    vtm_a = ar.alloc("vtm", 4096, hF)
    emtm_a = ar.alloc("emtm", 4096, hF)
    g["ktm"] = [ktm_a[:, 256 * t:256 * (t + 1)] for t in range(NT)]
    g["vtm"] = [vtm_a[:, 256 * t:256 * (t + 1)] for t in range(NT)]
    g["emtm"] = [emtm_a[:, 256 * t:256 * (t + 1)] for t in range(NT)]
    ttst_a = ar.alloc("ttst", 2048, hF)
    g["ttst"] = [ttst_a[:, 128 * c:128 * (c + 1)] for c in range(NCH)]
    g["vf"] = ar.alloc("vf", 4096, hF)
    g["emaf16"] = g["vf"]
    g["qpreP"] = [ar.alloc(f"qpre{i}", 2064, hF) for i in range(2)]
    g["diagP"] = [ar.alloc(f"diag{i}", 512, hF) for i in range(2)]
    g["convaccP"] = [ar.alloc(f"convacc{i}", 2048, hF) for i in range(2)]
    g["sqw"] = ar.alloc("sqw", 2048, hF)
    cwk = ar.alloc("cwk", 2048, hF)
    g["cwkP"] = [cwk[:, 128 * i:128 * (i + 1)] for i in range(6)]
    g["cwkQ"] = [cwk[:, 768 + 128 * i:768 + 128 * (i + 1)] for i in range(5)]
    g["cwkN"] = cwk[:, 1408:1536]
    g["cwkNT"] = cwk[:, 1536:1664]
    g["cwkTT"] = [cwk[:, 1664 + 128 * i:1664 + 128 * (i + 1)] for i in range(2)]
    g["cwkA"] = ar.alloc("cwkA", 128, hF)
    cwk2 = ar.alloc("cwk2", 2048, hF)
    g["cwkP2"] = [cwk2[:, 128 * i:128 * (i + 1)] for i in range(6)]
    g["cwkQ2"] = [cwk2[:, 768 + 128 * i:768 + 128 * (i + 1)] for i in range(5)]
    g["cwkN2"] = cwk2[:, 1408:1536]
    g["cwkNT2"] = cwk2[:, 1536:1664]
    g["cwkTT2"] = [cwk2[:, 1664 + 128 * i:1664 + 128 * (i + 1)] for i in range(2)]
    g["cwkA2"] = ar.alloc("cwkA2", 128, hF)
    g["sS"] = ar.alloc("sS", 512, hF)
    g["sf32"] = ar.alloc("sf32", 512, hF, F32)
    g["scw_r"] = ar.alloc("scw_r", 256, hF)
    g["scw_u"] = ar.alloc("scw_u", 256, hF)
    g["sqscr"] = ar.alloc("sqscr", 256, hF)
    g["ssqc"] = ar.alloc("ssqc", 48, hF, F32)
    g["nbcol"] = ar.alloc("nbcol", 16, hF, F32)
    ostc_a = ar.alloc("ostc", 4096, hF)
    g["ostc_all"] = ostc_a
    g["ostc"] = [ostc_a[:, 256 * t:256 * (t + 1)] for t in range(NT)]
    g["wqsP"] = [ar.alloc(f"wqs{i}", 2048, hF) for i in range(2)]
    g["wksP"] = [ar.alloc(f"wks{i}", 2048, hF) for i in range(2)]
    g["wvsP"] = [ar.alloc(f"wvs{i}", 2048, hF) for i in range(2)]
    g["gb_h"] = ar.alloc("gb_h", S, hF, F32)
    g["gd1"] = ar.alloc("gd1", S, hF, F32)
    lrow2 = ar.alloc("lrow", S, hF | {"p1", "p3"}, F32)
    g["lrowA"] = lrow2[0:1, 0:S]
    g["lrowB"] = lrow2[32:33, 0:S]

    g["of_fm8"] = ar.alloc("of_fm", 16384, {"p3"})
    g["of_fm"] = [g["of_fm8"][:, 2048 * f:2048 * (f + 1)] for f in range(8)]
    g["wos"] = ar.alloc("wos", 8192, {"p3"})
    g["x1f_str"] = [ar.alloc(f"x1f{i}", 2048, {"p3"}) for i in range(2)]
    g["lgf"] = ar.alloc("lgf", S, {"p3"}, F32)
    g["probsf"] = ar.alloc("probsf", S, {"p3"}, F32)
    g["lnsq"] = ar.alloc("lnsq", D, {"p3", "p5"}, F32)
    g["x1stage"] = ar.alloc("x1stage", 1024, {"p3", "p5"})
    g["ws_stage"] = ar.alloc("ws_stage", 128, {"p1"})
    g["cws"] = ar.alloc("cws", 64, {"p3"})

    h_fm_a = ar.alloc("h_fm", 8192, {"p4"})
    g["h_fm"] = [h_fm_a[:, 512 * m:512 * (m + 1)] for m in range(16)]
    g["xe_fmP"] = [ar.alloc(f"xe_fm{i}", 9 * KTOP, {"p4"}) for i in range(2)]
    g["gtmP"] = [ar.alloc(f"gtm{i}", 4 * XR, {"p4"}) for i in range(2)]
    g["growP"] = [ar.alloc(f"grow{i}", KTOP, {"p4"}) for i in range(2)]
    g["w1s2"] = [ar.alloc(f"w1s{i}", 1024, {"p4"}) for i in range(8)]
    w2all = ar.alloc("w2all", 16384, {"p4"})
    g["w2s"] = [w2all[:, 1024 * i:1024 * (i + 1)] for i in range(16)]
    g["yscat"] = ar.alloc("yscat", 4 * D, {"p4"}, F32)

    g["stg16"] = [ar.alloc(f"stg16_{i}", 1024, {"p0"}) for i in range(2)]
    g["zz16"] = ar.alloc("zz16", 2 * D, {"p0"})
    g["resw"] = [ar.alloc(f"resw{i}", D, {"p0", "p3", "p5"}, F32) for i in range(2)]
    g["outw"] = ar.alloc("outw", D, {"p5"}, F32)
    g["outwP"] = [g["outw"], ar.alloc("outw2", D, {"p5"}, F32)]
    g["x1stP"] = [ar.alloc(f"x1st{i}", 1024, {"p5"}) for i in range(2)]

    for n, shp, dt in (
        ("bgm", [16, S], F32), ("bgm16", [16, S], BF16), ("mixtm", [128, NT * 24], F32),
        ("onwb", [128, DK], BF16), ("c_maskSL", [128, 128], F32), ("c_maskSU", [128, 128], F32),
        ("c_maskSLn", [128, 128], F32), ("c_maskSUn", [128, 128], F32),
        ("c_maskIU", [128, 128], F32), ("c_identb", [128, 128], BF16), ("c_identf", [128, 128], F32),
        ("c_ones_b", [128, 1], BF16), ("c_ones_f", [128, 1], F32), ("c_ones_h", [128, 1], F16),
        ("c_onesr_f", [33, 128], F32),
        ("c_iota_d", [128, KTOP], F16), ("c_iota_tok", [128, 16], F16), ("c_neg1", [128, 16], F16),
        ("mixbias", [16, 1], F32), ("cwv", [128, 96], F32), ("probstm", [128, 128], F32),
        ("extr16", [128, 128], BF16), ("thrw", [1, 48], F32), ("mid128", [1, 128], F32),
        ("cmask", [128, 128], F16), ("cmaski", [128, 128], mybir.dt.uint8),
        ("selw", [1, 8], I32), ("selnw", [1, 8], I32), ("cand", [128, 16], F16), ("s16", [128, 16], F16),
        ("s16f", [128, 16], F32), ("kcol", [128, 1], F32), ("basec", [128, 2], F32),
        ("eqw", [128, KTOP], F16), ("Mplace", [128, KTOP], F16), ("idxrow", [1, KTOP], F32),
        ("idxc", [128, 4 * E], I32), ("gcol", [128, 4 * E], F32), ("grow", [1, KTOP], BF16),
        ("omix", [128, DK], F32), ("otmp", [128, DK], F32), ("colw", [128, 16], F32),
        ("c_eps", [128, 4], F32),
    ):
        g[n] = sbuf(n, shp, dt)

    g["PS"] = [psum(f"ps{i}", [128, 512], F32) for i in range(6)]
    g["PSB"] = psum("psb", [128, 512], BF16)
    g["PSC"] = psum("psc", [128, 512], F32)

    nc.const_aps.register(F32, EPS_L2, g["c_eps"][:, 0:1])
    nc.const_aps.register(F32, EPS_RMS, g["c_eps"][:, 1:2])
    nc.const_aps.register(F32, EPS_LN, g["c_eps"][:, 2:3])
    ctx = Ctx(nc)
    g["ctx"] = ctx
    g["wqkv_tok"] = {}
    g["xesd"] = [g["xes0"], g["xes1"]]
    with st:
        for e in ENGS:
            ctx.sem[e] = st.enter_context(nc.semaphore("sem" + e))
        g["wc_sem"] = st.enter_context(nc.semaphore("semWC"))
        with nc.Block() as blk:
            g["blk"] = blk
            seq = ["p0", "p1", "h0", "h1", "h2", "h3", "p3", "p4", "p5"]
            stop = seq.index(upto)
            with nc.named_scope("p0"):
                _phase0(g)
            if stop >= 1:
                with nc.named_scope("p1"):
                    _phase1(g)
            for h in range(H):
                if stop >= 2 + h:
                    with nc.named_scope(f"head{h}"):
                        _head(g, h)
            if stop >= 6:
                with nc.named_scope("p3"):
                    _phase3(g)
            if stop >= 7:
                with nc.named_scope("p4"):
                    _phase4(g)
            if stop >= 8:
                with nc.named_scope("p5"):
                    _phase5(g)
            ctx2 = g["ctx"]
            @blk.sync
            def _(sy):
                for e in ENGS:
                    if e != "SY" and ctx2.cnt[e] > 0:
                        sy.wait_ge(ctx2.sem[e], ctx2.cnt[e])
                sy.wait_ge(ctx2.sem["SY"], ctx2.cnt["SY"])
    return nc, DBG


def _dump_rows(g, name, src_f32):
    if name not in g["DBG"]:
        return
    ctx, blk = g["ctx"], g["blk"]
    dst = g["DBG"][name]

    @blk.sync
    def _(sy):
        ctx.wait(sy, "SY", ctx.barrier_toks())
        ctx.dma(sy, "SY", out=dst, in_=src_f32)


def _dump16(g, name, src_bf16_2ct):
    if name not in g["DBG"]:
        return
    ctx, blk = g["ctx"], g["blk"]
    dst = g["DBG"][name]
    for ct in range(2):
        @blk.vector
        def _(v, ct=ct):
            ctx.wait(v, "V", ctx.barrier_toks())
            ctx.inc(v.tensor_copy(out=g["gd1"][:, :],
                                  in_=src_bf16_2ct[:, 2048 * ct:2048 * (ct + 1)]), "V")

        @blk.sync
        def _(sy, ct=ct):
            ctx.wait(sy, "SY", ctx.tok("V"))
            ctx.dma(sy, "SY", out=dst[128 * ct:128 * (ct + 1), :], in_=g["gd1"][:, :])


def _phase0(g):
    ctx, blk = g["ctx"], g["blk"]

    @blk.sync
    def _(sy):
        for nm in g["cnames"]:
            ctx.dma(sy, "SY", out=g[nm][:, :], in_=g["cdram"][nm][:, :])
        ctx.dma(sy, "SY", out=g["cwv"][:, :].rearrange("p (hc w) -> p hc w", hc=8),
                in_=g["convw"].rearrange("(hc p) w -> p hc w", p=128))

    @blk.vector
    def _(v):
        ctx.inc(v.memset(g["zz16"][:, :], 0.0), "V")
        ctx.inc(v.memset(g["idxc"][:, :], 0), "V")

    for t in range(NT):
        @blk.sync
        def _(sy, t=t):
            if t >= 2:
                ctx.wait(sy, "SY", ctx.tok("GP"))
            ctx.dma(sy, "SY", out=g["resw"][t % 2][:, :], in_=g["hs"][128 * t:128 * (t + 1), :])

        @blk.vector
        def _(v, t=t):
            ctx.wait(v, "V", ctx.tok("SY"))
            ctx.inc(v.tensor_copy(out=g["stg16"][t % 2][:, :], in_=g["resw"][t % 2][:, :]), "V")

        @blk.gpsimd
        def _(gp, t=t):
            ctx.wait(gp, "GP", ctx.tok("V"))
            ctx.dma(gp, "GP", out=g["hs16"][128 * t:128 * (t + 1), :], in_=g["stg16"][t % 2][:, :])

    @blk.gpsimd
    def _(gp):
        zzf = g["zz16"].bitcast(F32)
        for i in range(NT):
            ctx.dma(gp, "GP", out=g["accd"].rearrange("(a p) d -> a p d", p=128)[i],
                                 in_=zzf[:, 0:1024])
        ctx.wait(gp, "GP", ctx.tok("V"))
        ctx.dma(gp, "GP", out=g["x1s"].rearrange("(t p) r -> p t r", p=128)[:, :, D + E:XR],
                in_=g["zz16"][:, 0:NT * (XR - D - E)].rearrange("p (t r) -> p t r", t=NT))

    @blk.sync
    def _(sy):
        ctx.wait(sy, "SY", ctx.tok("GP"))
        for f in range(8):
            ctx.dma(sy, "SY", out=g["hs_fm"][f], in_=g["hs16"][:, 128 * f:128 * (f + 1)],
                                 transpose=True)
    g["hsfm_tok"] = ctx.tok("SY")
    g["cw_tok0"] = ("SY", 16 * len(g["cnames"]) + 16)


def _phase1(g):
    ctx, blk = g["ctx"], g["blk"]
    PS = g["PS"]

    @blk.vector
    def _(v):
        ctx.inc(v.memset(g["mixbias"][:, :], 0.0), "V")

    @blk.sync
    def _(sy):
        ctx.wait(sy, "SY", ctx.tok("V"))
        ctx.dma(sy, "SY", out=g["mixbias"][8:12, 0:1],
                             in_=g["mixb"][0:1, :].rearrange("a (b c) -> (a b) c", c=1))

    @blk.gpsimd
    def _(gp):
        ctx.dma(gp, "GP", out=g["ws_stage"][:, :].rearrange("p (kk m) -> p kk m", kk=8),
                             in_=g["wsT"].rearrange("(kk p) m -> p kk m", p=128))
    pre = [ctx.tok("GP"), ctx.tok("SY"), g["hsfm_tok"]]

    for nh in range(4):
        @blk.tensor
        def _(T, nh=nh):
            ctx.wait(T, "T", pre + [ctx.tok("SC")])
            pb = PS[nh % 2]
            for f in range(8):
                _li = T.matmul(pb[0:16, 0:512],
                         g["ws_stage"][:, :].rearrange("p (kk m) -> p kk m", kk=8)[:, f, :],
                         g["hs_fm"][f][:, 512 * nh:512 * (nh + 1)], start=(f == 0), stop=(f == 7))
            ctx.inc(_li, "T", 1)

        @blk.scalar
        def _(sc, nh=nh):
            ctx.wait(sc, "SC", ctx.tok("T"))
            ctx.inc(sc.activation(out=g["bgm"][:, 512 * nh:512 * (nh + 1)],
                                  in_=PS[nh % 2][0:16, 0:512], func=AF.Sigmoid,
                                  bias=g["mixbias"][:, 0:1]), "SC")

    @blk.vector
    def _(v):
        ctx.wait(v, "V", ctx.tok("SC"))
        ctx.inc(v.tensor_copy(out=g["bgm16"][:, :], in_=g["bgm"][:, :]), "V")

    for t in range(NT):
        @blk.tensor
        def _(T, t=t):
            ctx.wait(T, "T", ctx.tok("V"))
            _li = T.transpose(g["PSB"][0:128, 0:16], g["bgm16"][0:16, 128 * t:128 * (t + 1)],
                        g["c_identb"][0:16, 0:16])
            ctx.inc(_li, "T", 1)

        @blk.vector
        def _(v, t=t):
            ctx.wait(v, "V", ctx.tok("T"))
            ctx.inc(v.tensor_copy(out=g["mixtm"][:, 24 * t:24 * t + 16], in_=g["PSB"][0:128, 0:16]), "V")

    _dump_rows(g, "bgm", g["bgm"][:, :])

    @blk.sync
    def _(sy):
        ctx.dma(sy, "SY", out=g["lrowA"][0:1, 0:DK], in_=g["onw"][:, :])

    @blk.tensor
    def _(T):
        ctx.wait(T, "T", [ctx.tok("SY"), ctx.tok("V")])
        ctx.inc(T.matmul(g["PSC"][:, 0:DK], g["c_onesr_f"][0:1, :], g["lrowA"][0:1, 0:DK],
                         start=True, stop=True), "T")

    @blk.vector
    def _(v):
        ctx.wait(v, "V", ctx.tok("T"))
        ctx.inc(v.tensor_copy(out=g["onwb"][:, :], in_=g["PSC"][:, 0:DK]), "V")
    g["psb_tok"] = ctx.tok("V")


def _wcast_chunk(g, h):
    ctx, blk = g["ctx"], g["blk"]

    @blk.gpsimd
    def _(gp):
        if h + 1 < H:
            ctx.wait(gp, "GP", g["proj_mm_tok"])
            par = (h + 1) % 2
            for wd, wl in (("wqT", "wqsP"), ("wkT", "wksP"), ("wvT", "wvsP")):
                ctx.dma(gp, "GP",
                    out=g[wl][par][:, :].rearrange("p (kk m) -> p kk m", kk=8),
                    in_=g[wd][:, 256 * (h + 1):256 * (h + 2)].rearrange("(kk p) m -> p kk m", p=128))
        g["wqkv_tok"][h + 1] = ctx.tok("GP")
        for e in (2 * h, 2 * h + 1):
            gp.dma_start(out=g["w116"][e][:, :], in_=g["w1"][e][:, :]).then_inc(g["wc_sem"], 16)
            gp.dma_start(out=g["w216"][e][:, :], in_=g["w2"][e][:, :]).then_inc(g["wc_sem"], 16)
    g["n_wcast"] = g.get("n_wcast", 0) + 4


def _head(g, h):
    nc = g["nc"]
    with nc.named_scope(f"proj{h}"):
        _head_proj(g, h)
    with nc.named_scope(f"ema{h}"):
        _head_ema(g, h)
    _wcast_chunk(g, h)
    with nc.named_scope(f"tph{h}"):
        _head_tphase(g, h)
        _head_vtmscale(g, h)
    with nc.named_scope(f"scan{h}"):
        _head_scan(g, h)
    _head_ost_write(g, h)


def _head_vtmscale(g, h):
    ctx, blk = g["ctx"], g["blk"]

    @blk.vector
    def _(v):
        for c in range(NCH):
            ctx.inc(v.tensor_scalar(out=g["vtm"][c], in0=g["vtm"][c],
                                    scalar1=g["mixtm"][:, 24 * c + h:24 * c + h + 1],
                                    scalar2=None, op0=ALU.mult), "V")


def _head_tmx(g, h, kind, cpair):
    ctx, blk = g["ctx"], g["blk"]
    src, dst, stok = {
        "k": (g["kf"], g["ktm"], g.get("l2_tok")),
        "v": (g["vf"], g["vtm"], g.get("_last_silu")),
        "em": (g["emaf16"], g["emtm"], g.get("ema_tok")),
    }[kind]
    c0 = 2 * cpair

    @blk.tensor
    def _(T, src=src, stok=stok, c0=c0):
        w = []
        if g.get("psb_tok"):
            w.append(g["psb_tok"])
        if stok:
            w.append(stok)
        ctx.wait(T, "T", w)
        for j, (c, ct) in enumerate(((c0, 0), (c0, 1), (c0 + 1, 0), (c0 + 1, 1))):
            _li = T.transpose(g["PSB"][:, 128 * j:128 * (j + 1)],
                              src[:, 2048 * ct + 128 * c:2048 * ct + 128 * (c + 1)],
                              g["c_identb"][:, :])
        ctx.inc(_li, "T", 1)
    tt = ctx.tok("T")

    @blk.vector
    def _(v, dst=dst, c0=c0, tt=tt):
        ctx.wait(v, "V", tt)
        ctx.inc(v.tensor_copy(out=dst[c0], in_=g["PSB"][:, 0:256]), "V")
        ctx.inc(v.tensor_copy(out=dst[c0 + 1], in_=g["PSB"][:, 256:512]), "V")
    g["psb_tok"] = ctx.tok("V")


def _head_proj(g, h):
    ctx, blk = g["ctx"], g["blk"]
    PS = g["PS"]
    par_h = h % 2
    specs = [("wqT", g["wqsP"][par_h], g["qf"], 0, True),
             ("wkT", g["wksP"][par_h], g["kf"], 1, True),
             ("wvT", g["wvsP"][par_h], g["vf"], 2, False)]

    @blk.vector
    def _(v):
        if h == 0:
            ctx.wait(v, "V", ctx.barrier_toks())
        ctx.inc(v.memset(g["qpreP"][0][:, 0:8], 0.0), "V")
        ctx.inc(v.memset(g["qpreP"][1][:, 0:8], 0.0), "V")


    if h == 0:
        @blk.gpsimd
        def _(gp):
            ctx.wait(gp, "GP", ctx.barrier_toks())
            for wd, wsl, _, _, _ in specs:
                ctx.dma(gp, "GP",
                    out=wsl[:, :].rearrange("p (kk m) -> p kk m", kk=8),
                    in_=g[wd][:, 0:256].rearrange("(kk p) m -> p kk m", p=128))
        wtok = ctx.tok("GP")
    else:
        wtok = g["wqkv_tok"][h]

    cwtok = g.get("cw_tok0") or ctx.tok("SY")

    prev_evac = {}
    conv_done = {}
    diag_v = {}
    silu_sc = {}
    ev_prev = {}
    prev_silu = g.get("_last_silu")
    pend = {"st": None}

    def emit_conv_step():
        st = pend["st"]
        if st is None or st["nh"] >= 4:
            return
        idx, par, ct, dst, nh = st["idx"], st["par"], st["ct"], st["dst"], st["nh"]

        @blk.tensor
        def _(T, idx=idx, par=par, nh=nh):
            w = [diag_v[idx], prev_evac[idx]]
            if nh >= 2:
                w.append(silu_sc[(idx, nh - 2)])
            elif (idx - 1, nh + 2) in silu_sc:
                w.append(silu_sc[(idx - 1, nh + 2)])
            elif prev_silu:
                w.append(prev_silu)
            ctx.wait(T, "T", w)
            pb = PS[4 + nh % 2]
            for j in range(4):
                _li = T.matmul(pb[:, 0:512], g["diagP"][par][:, 128 * j:128 * (j + 1)],
                         g["qpreP"][par][:, 8 - (3 - j) + 512 * nh:8 - (3 - j) + 512 * nh + 512],
                         start=(j == 0), stop=(j == 3))
            ctx.inc(_li, "T", 1)
        cmt = ctx.tok("T")

        @blk.scalar
        def _(sc, idx=idx, par=par, ct=ct, dst=dst, nh=nh, cmt=cmt):
            ctx.wait(sc, "SC", cmt)
            ctx.inc(sc.activation(out=dst[:, 2048 * ct + 512 * nh:2048 * ct + 512 * (nh + 1)],
                                  in_=PS[4 + nh % 2][:, 0:512], func=AF.Silu), "SC")
        silu_sc[(idx, nh)] = ctx.tok("SC")
        st["nh"] += 1
        if st["nh"] >= 4:
            conv_done[idx] = cmt
            g["_last_silu"] = silu_sc[(idx, 3)]

    idx = 0
    for wd, wsl, dst, ci, do_norm in specs:
        for ct in range(2):
            par = idx % 2
            hc = 2 * h + ct
            @blk.vector
            def _(v, ci=ci, hc=hc, par=par, idx=idx):
                ctx.wait(v, "V", cwtok)
                tk = conv_done.get(idx - 2)
                if tk:
                    ctx.wait(v, "V", tk)
                for j in range(4):
                    ctx.inc(v.tensor_scalar(out=g["diagP"][par][:, 128 * j:128 * (j + 1)],
                                            in0=g["c_identb"][:, :],
                                            scalar1=g["cwv"][:, 12 * hc + 4 * ci + j:12 * hc + 4 * ci + j + 1],
                                            scalar2=None, op0=ALU.mult), "V")
            diag_v[idx] = ctx.tok("V")

            ev = {}
            for nh in range(4):
                @blk.tensor
                def _(T, ct=ct, nh=nh, wsl=wsl, idx=idx, ev=ev):
                    w = [wtok]
                    tk2 = ev_prev.get((idx - 1, nh))
                    if tk2:
                        w.append(tk2)
                    if idx == 0:
                        if h == 0 and nh == 0:
                            w.append(ctx.tok("V"))
                        elif nh >= 2:
                            w.append(diag_v[0])
                    ctx.wait(T, "T", w)
                    pb = PS[nh]
                    for kk in range(8):
                        _li = T.matmul(pb[:, 0:512],
                                 wsl[:, :].rearrange("p (kk m) -> p kk m", kk=8)[:, kk, 128 * ct:128 * (ct + 1)],
                                 g["hs_fm"][kk][:, 512 * nh:512 * (nh + 1)],
                                 start=(kk == 0), stop=(kk == 7))
                    ctx.inc(_li, "T", 1)
                nh_t = ctx.tok("T")

                @blk.vector
                def _(v, nh=nh, nh_t=nh_t, par=par, idx=idx):
                    ctx.wait(v, "V", nh_t)
                    if nh == 0:
                        tk = conv_done.get(idx - 2)
                        if tk:
                            ctx.wait(v, "V", tk)
                    ctx.inc(v.tensor_copy(out=g["qpreP"][par][:, 8 + 512 * nh:8 + 512 * (nh + 1)],
                                          in_=PS[nh][:, 0:512]), "V")
                ev[nh] = ev_prev[(idx, nh)] = ctx.tok("V")
                emit_conv_step()
            prev_evac[idx] = ev[3]
            pend["st"] = {"idx": idx, "par": par, "ct": ct, "dst": dst, "nh": 0}
            idx += 1
    for _ in range(4):
        emit_conv_step()

    g["proj_mm_tok"] = ctx.tok("T")

    stok = _l2norm(g, g["qf"], prev_evac[5], None)
    stok = _l2norm(g, g["kf"], prev_evac[5], stok)
    g["l2_tok"] = stok

    if h == 0:
        _dump16(g, "v", g["vf"])
        _dump16(g, "qn", g["qf"])
        _dump16(g, "kn", g["kf"])


def _l2norm(g, dst, vcover, prev_scale):
    ctx, blk = g["ctx"], g["blk"]
    PS = g["PS"]
    sum_t = {}
    for ct in range(2):
        @blk.scalar
        def _(sc, ct=ct):
            if ct > 0:
                ctx.wait(sc, "SC", sum_t[ct - 1])
            ctx.inc(sc.activation(out=g["sqw"][:, :], in_=dst[:, 2048 * ct:2048 * (ct + 1)],
                                  func=AF.Square), "SC")
        sq_t = ctx.tok("SC")

        @blk.tensor
        def _(T, ct=ct, sq_t=sq_t):
            w = [sq_t]
            if ct == 0:
                w.append(vcover)
                if prev_scale:
                    w.append(prev_scale)
            ctx.wait(T, "T", w)
            for nh in range(4):
                _li = T.matmul(PS[2 + nh][0:1, 0:512], g["c_ones_b"][:, :],
                         g["sqw"][:, 512 * nh:512 * (nh + 1)], start=(ct == 0), stop=(ct == 1))
            ctx.inc(_li, "T", 1)
        sum_t[ct] = ctx.tok("T")

    @blk.scalar
    def _(sc):
        ctx.wait(sc, "SC", sum_t[1])
        for nh in range(4):
            ctx.inc(sc.activation(out=g["lrowA"][0:1, 512 * nh:512 * (nh + 1)],
                                  in_=PS[2 + nh][0:1, 0:512], func=AF.Sqrt, bias=EPS_L2), "SC")
    sqrt_t = ctx.tok("SC")

    @blk.vector
    def _(v):
        ctx.wait(v, "V", sqrt_t)
        ctx.inc(v.reciprocal(out=g["lrowB"][0:1, :], in_=g["lrowA"][0:1, :]), "V")
    rtok = ctx.tok("V")

    @blk.tensor
    def _(T):
        ctx.wait(T, "T", rtok)
        for nh in range(4):
            _li = T.matmul(PS[2 + nh][:, 0:512], g["c_onesr_f"][32:33, :],
                     g["lrowB"][0:1, 512 * nh:512 * (nh + 1)], start=True, stop=True)
        ctx.inc(_li, "T", 1)
    bc_t = ctx.tok("T")

    @blk.vector
    def _(v):
        ctx.wait(v, "V", bc_t)
        for ct in range(2):
            for nh in range(4):
                sl = dst[:, 2048 * ct + 512 * nh:2048 * ct + 512 * (nh + 1)]
                ctx.inc(v.tensor_tensor(out=sl, in0=sl, in1=PS[2 + nh][:, 0:512], op=ALU.mult), "V")
    return ctx.tok("V")


def _head_vbounce(g, h):
    ctx, blk = g["ctx"], g["blk"]

    @blk.gpsimd
    def _(gp):
        ctx.wait(gp, "GP", ctx.tok("SC"))
        ctx.dma(gp, "GP", out=g["sc16"][DK:2 * DK, :].rearrange("(ct p) s -> p ct s", p=128),
                             in_=g["vf"][:, :].rearrange("p (ct s) -> p ct s", ct=2))
    g["vb_tok"] = ctx.tok("GP")


def _head_ema(g, h):
    ctx, blk = g["ctx"], g["blk"]
    PS = g["PS"]
    for cp in range(NCH // 2):
        _head_tmx(g, h, "v", cp)

    @blk.sync
    def _(sy):
        ctx.wait(sy, "SY", [ctx.tok("V"), ctx.tok("T")])
        ctx.dma(sy, "SY", out=g["lrowA"][0:1, :], in_=g["bgm"][4 + h:5 + h, :])
        ctx.dma(sy, "SY", out=g["lrowB"][0:1, :], in_=g["bgm"][0 + h:1 + h, :])
    g["brow_tok"] = ctx.tok("SY")

    @blk.tensor
    def _(T):
        ctx.wait(T, "T", [ctx.tok("V"), ctx.tok("SC"), g["brow_tok"]])
        for nh in range(4):
            _li = T.matmul(PS[2 + nh][:, 0:512], g["c_onesr_f"][0:1, :],
                     g["lrowA"][0:1, 512 * nh:512 * (nh + 1)], start=True, stop=True)
        ctx.inc(_li, "T", 1)

    @blk.vector
    def _(v):
        ctx.wait(v, "V", ctx.tok("T"))
        for nh in range(4):
            ctx.inc(v.tensor_copy(out=g["gb_h"][:, 512 * nh:512 * (nh + 1)], in_=PS[2 + nh][:, 0:512]), "V")

    for ct in range(2):
        @blk.vector
        def _(v, ct=ct):
            ctx.sw(v, "V")
            ctx.inc(v.tensor_scalar(out=g["gd1"][:, :], in0=g["gb_h"][:, :], scalar1=-1.0,
                                    scalar2=1.0, op0=ALU.mult, op1=ALU.add), "V")
            ctx.sw(v, "V")
            ctx.inc(v.tensor_tensor(out=g["gd1"][:, :], in0=g["gd1"][:, :],
                                    in1=g["vf"][:, 2048 * ct:2048 * (ct + 1)], op=ALU.mult), "V")
            ctx.sw(v, "V")
            ctx.inc(v.tensor_tensor_scan(out=g["emaf16"][:, 2048 * ct:2048 * (ct + 1)],
                                         data0=g["gb_h"][:, :], data1=g["gd1"][:, :],
                                         initial=0.0, op0=ALU.mult, op1=ALU.add), "V")
    g["ema_tok"] = ctx.tok("V")
    if h == 0:
        _dump16(g, "ema", g["emaf16"])


def _head_tphase(g, h):
    ctx, blk = g["ctx"], g["blk"]
    PS = g["PS"]
    vt = {}
    tt = {}

    def emitT(c, step, fn):
        @blk.tensor
        def _(T, c=c, step=step):
            if step == 0:
                if c >= 2 and (c - 2, 13) in vt:
                    ctx.wait(T, "T", vt[(c - 2, 13)])
                else:
                    ctx.wait(T, "T", ctx.tok("V"))
            else:
                ctx.wait(T, "T", vt[(c, step - 1)])
            fn(T)
        tt[(c, step)] = ctx.tok("T")

    def emitV(c, step, fn):
        @blk.vector
        def _(v, c=c, step=step):
            ctx.wait(v, "V", tt[(c, step)])
            ctx.sw(v, "V")
            fn(v, c)
        vt[(c, step)] = ctx.tok("V")

    W = [{"P": g["cwkP"], "Q": g["cwkQ"], "N": g["cwkN"], "NT": g["cwkNT"],
          "TT": g["cwkTT"], "A": g["cwkA"]},
         {"P": g["cwkP2"], "Q": g["cwkQ2"], "N": g["cwkN2"], "NT": g["cwkNT2"],
          "TT": g["cwkTT2"], "A": g["cwkA2"]}]

    def chunk_steps(c):
        pb = PS[c % 2]
        w = W[c % 2]
        steps = []

        def s0(T, c=c, pb=pb):
            for kt in range(2):
                _li = T.matmul(pb[:, 0:128],
                         g["kf"][:, 2048 * kt + 128 * c:2048 * kt + 128 * (c + 1)],
                         g["kf"][:, 2048 * kt + 128 * c:2048 * kt + 128 * (c + 1)],
                         start=(kt == 0), stop=(kt == 1))
            _li = T.matmul(pb[:, 128:256], g["c_onesr_f"][32:33, :],
                     g["lrowB"][0:1, 128 * c:128 * (c + 1)], start=True, stop=True)
            ctx.inc(_li, "T", 1)
        steps.append(("T", s0))

        def v0(v, c, pb=pb, w=w):
            ctx.inc(v.scalar_tensor_tensor(out=w["N"], in0=pb[:, 0:128],
                                           scalar=g["mixtm"][:, 24 * c + h:24 * c + h + 1],
                                           in1=g["c_maskSLn"][:, :], op0=ALU.mult, op1=ALU.mult), "V")
            ctx.inc(v.tensor_tensor(out=w["A"], in0=pb[:, 0:128], in1=g["c_maskSUn"][:, :],
                                    op=ALU.mult), "V")
            ctx.sw(v, "V")
            ctx.inc(v.tensor_tensor(out=w["NT"], in0=w["A"], in1=pb[:, 128:256], op=ALU.mult), "V")
            ctx.sw(v, "V")
            ctx.inc(v.tensor_tensor(out=w["TT"][0], in0=w["NT"], in1=g["c_identb"][:, :],
                                    op=ALU.add), "V")
        steps.append(("V", v0))

        tt_par = [0]
        Pp = [w["N"], w["NT"]]
        for j in range(1, 7):
            def tj(T, c=c, pb=pb, j=j, P=Pp[0], Q=Pp[1]):
                _li = T.matmul(pb[:, 256:384], Q, P, start=True, stop=True)
                if j <= 5:
                    _li = T.matmul(pb[:, 384:512], P, Q, start=True, stop=True)
                ctx.inc(_li, "T", 1)
            steps.append(("T", tj))

            def vj(v, c, pb=pb, w=w, j=j):
                ctx.inc(v.tensor_copy(out=w["P"][j - 1], in_=pb[:, 256:384]), "V")
                if j <= 5:
                    ctx.inc(v.tensor_copy(out=w["Q"][j - 1], in_=pb[:, 384:512]), "V")
            steps.append(("V", vj))

            def tj2(T, c=c, pb=pb, w=w, j=j, par=tt_par[0]):
                ctx.inc(T.matmul(pb[:, 256:384], w["P"][j - 1], w["TT"][par],
                                 start=True, stop=True), "T")
            steps.append(("T", tj2))

            def vj2(v, c, pb=pb, w=w, j=j, par=tt_par[0]):
                dst = g["ttst"][c] if j == 6 else w["TT"][1 - par]
                ctx.inc(v.tensor_tensor(out=dst, in0=w["TT"][par], in1=pb[:, 256:384],
                                        op=ALU.add), "V")
            steps.append(("V", vj2))
            Pp = [w["P"][j - 1], (w["Q"][j - 1] if j <= 5 else None)]
            tt_par[0] = 1 - tt_par[0]
        return steps

    for cbase in range(0, NCH, 2):
        _head_tmx(g, h, "k", cbase // 2)
        sa = chunk_steps(cbase)
        sb = chunk_steps(cbase + 1)
        ia = ib = 0
        ka = kb = 0
        while ia < len(sa) or ib < len(sb):
            if ia == 12 and sa[ia][0] == "T":
                _head_tmx(g, h, "em", cbase // 2)
            if ia < len(sa):
                kind, fn = sa[ia]
                (emitT if kind == "T" else emitV)(cbase, ka if kind == "T" else ka, fn)
                if kind == "V":
                    ka += 1
                ia += 1
            if ib < len(sb):
                kind, fn = sb[ib]
                (emitT if kind == "T" else emitV)(cbase + 1, kb if kind == "T" else kb, fn)
                if kind == "V":
                    kb += 1
                ib += 1

    if h == 0 and "tt0" in g["DBG"]:
        @blk.vector
        def _(v):
            ctx.wait(v, "V", ctx.barrier_toks())
            ctx.inc(v.tensor_copy(out=g["gd1"][:, 0:128], in_=g["ttst"][0]), "V")
        _dump_rows(g, "tt0", g["gd1"][:, 0:128])


def _head_scan(g, h):
    ctx, blk = g["ctx"], g["blk"]
    PS = g["PS"]
    v3_tok = {}

    @blk.vector
    def _(v):
        for c in range(NCH):
            ctx.inc(v.tensor_scalar_mul(g["nbcol"][:, c:c + 1],
                                        g["mixtm"][:, 24 * c + h:24 * c + h + 1], -1.0), "V")

    for c in range(NCH):
        @blk.tensor
        def _(T, c=c):
            ctx.wait(T, "T", v3_tok[c - 1] if c > 0 else ctx.tok("V"))
            _li = None
            if c > 0:
                for kt in range(2):
                    _li = T.matmul(PS[3][:, 0:256],
                             g["kf"][:, 2048 * kt + 128 * c:2048 * kt + 128 * (c + 1)],
                             g["sS"][:, 256 * kt:256 * (kt + 1)], start=(kt == 0), stop=(kt == 1))
            ctx.inc(_li if _li is not None else T.nop(), "T", 1)

        @blk.vector
        def _(v, c=c):
            ctx.wait(v, "V", ctx.tok("T"))
            if c == 0 and g.get("ost_tok"):
                ctx.wait(v, "V", g["ost_tok"])
            if c > 0:
                ctx.inc(v.scalar_tensor_tensor(
                    out=g["scw_r"], in0=PS[3][:, 0:256],
                    scalar=g["nbcol"][:, c:c + 1],
                    in1=g["vtm"][c], op0=ALU.mult, op1=ALU.add), "V")
            else:
                ctx.inc(v.tensor_copy(out=g["scw_r"], in_=g["vtm"][c]), "V")

        @blk.tensor
        def _(T, c=c):
            ctx.wait(T, "T", ctx.tok("V"))
            _li = T.matmul(PS[3][:, 256:512], g["ttst"][c], g["scw_r"], start=True, stop=True)
            for kt in range(2):
                _li = T.matmul(PS[4][:, 0:128],
                         g["kf"][:, 2048 * kt + 128 * c:2048 * kt + 128 * (c + 1)],
                         g["qf"][:, 2048 * kt + 128 * c:2048 * kt + 128 * (c + 1)],
                         start=(kt == 0), stop=(kt == 1))
            ctx.inc(_li, "T", 1)

        @blk.vector
        def _(v, c=c):
            ctx.wait(v, "V", ctx.tok("T"))
            ctx.inc(v.tensor_copy(out=g["scw_u"], in_=PS[3][:, 256:512]), "V")
            ctx.inc(v.tensor_tensor(out=g["cwkA"], in0=PS[4][:, 0:128], in1=g["c_maskIU"][:, :],
                                    op=ALU.mult), "V")

        @blk.tensor
        def _(T, c=c):
            ctx.wait(T, "T", ctx.tok("V"))
            first = True
            if c > 0:
                for kt in range(2):
                    _li = T.matmul(PS[4][:, 128:384],
                             g["qf"][:, 2048 * kt + 128 * c:2048 * kt + 128 * (c + 1)],
                             g["sS"][:, 256 * kt:256 * (kt + 1)], start=(kt == 0), stop=False)
                first = False
            _li = T.matmul(PS[4][:, 128:384], g["cwkA"], g["scw_u"], start=first, stop=True)
            for kt in range(2):
                _li = T.matmul((PS[2], PS[5])[kt][:, 0:256], g["ktm"][c][:, 128 * kt:128 * (kt + 1)],
                         g["scw_u"], start=True, stop=True)
            ctx.inc(_li, "T", 1)

        @blk.vector
        def _(v, c=c):
            ctx.wait(v, "V", ctx.tok("T"))
            if c == 0:
                ctx.inc(v.tensor_copy(out=g["sf32"][:, 0:256], in_=PS[2][:, 0:256]), "V")
                ctx.inc(v.tensor_copy(out=g["sf32"][:, 256:512], in_=PS[5][:, 0:256]), "V")
            else:
                ctx.inc(v.tensor_tensor(out=g["sf32"][:, 0:256], in0=g["sf32"][:, 0:256],
                                        in1=PS[2][:, 0:256], op=ALU.add), "V")
                ctx.inc(v.tensor_tensor(out=g["sf32"][:, 256:512], in0=g["sf32"][:, 256:512],
                                        in1=PS[5][:, 0:256], op=ALU.add), "V")
            ctx.sw(v, "V")
            ctx.inc(v.tensor_copy(out=g["sS"][:, :], in_=g["sf32"][:, :]), "V")
        v3_tok[c] = ctx.tok("V")

        @blk.vector
        def _(v, c=c):
            ctx.sw(v, "V")
            ctx.inc(v.tensor_tensor(out=g["otmp"][:, :], in0=g["emtm"][c], in1=PS[4][:, 128:384],
                                    op=ALU.subtract), "V")
            ctx.sw(v, "V")
            ctx.inc(v.scalar_tensor_tensor(
                out=g["omix"][:, :], in0=g["otmp"][:, :],
                scalar=g["mixtm"][:, 24 * c + 8 + h:24 * c + 9 + h],
                in1=PS[4][:, 128:384], op0=ALU.mult, op1=ALU.add), "V")
            ctx.sw(v, "V")
            ctx.inc(v.tensor_copy(out=g["ostc"][c], in_=g["omix"][:, :]), "V")
        omix_tok = ctx.tok("V")

        @blk.scalar
        def _(sc, c=c, ost_v=omix_tok):
            ctx.wait(sc, "SC", ost_v)
            ctx.inc(sc.activation(out=g["sqscr"], in_=g["ostc"][c], func=AF.Square,
                                  accum_out=g["ssqc"][:, c:c + 1]), "SC")
    sq_all = ctx.tok("SC")

    @blk.scalar
    def _(sc):
        ctx.wait(sc, "SC", sq_all)
        ctx.inc(sc.activation(out=g["ssqc"][:, 16:32], in_=g["ssqc"][:, 0:16],
                              func=AF.Sqrt, scale=1.0 / DK, bias=EPS_RMS), "SC")
    rms_sc = ctx.tok("SC")

    @blk.vector
    def _(v):
        ctx.wait(v, "V", rms_sc)
        ctx.inc(v.reciprocal(out=g["ssqc"][:, 32:48], in_=g["ssqc"][:, 16:32]), "V")
        ctx.sw(v, "V")
        for c in range(NCH):
            ctx.inc(v.scalar_tensor_tensor(
                out=g["ostc"][c], in0=g["ostc"][c], scalar=g["ssqc"][:, 32 + c:33 + c],
                in1=g["onwb"][:, :], op0=ALU.mult, op1=ALU.mult), "V")


def _head_ost_write(g, h):
    ctx, blk = g["ctx"], g["blk"]

    @blk.sync
    def _(sy):
        ctx.wait(sy, "SY", ctx.tok("V"))
        ctx.dma(sy, "SY", 
            out=g["ost16"].rearrange("(t p) d -> p t d", p=128)[:, :, 256 * h:256 * (h + 1)],
            in_=g["ostc_all"][:, :].rearrange("p (t d) -> p t d", t=NT))
    g["ost_tok"] = ctx.tok("SY")


def _phase3(g):
    ctx, blk = g["ctx"], g["blk"]
    PS = g["PS"]

    @blk.gpsimd
    def _(gp):
        ctx.wait(gp, "GP", ctx.barrier_toks())
        ctx.dma(gp, "GP", out=g["wos"][:, :].rearrange("p (kk m) -> p kk m", kk=8),
                             in_=g["woT"].rearrange("(kk p) m -> p kk m", p=128))
    wo_tok = ctx.tok("GP")

    @blk.sync
    def _(sy):
        ctx.wait(sy, "SY", [g["ost_tok"], ctx.tok("GP")])
        for f in range(8):
            ctx.dma(sy, "SY", out=g["of_fm"][f], in_=g["ost16"][:, 128 * f:128 * (f + 1)],
                                 transpose=True)
    of_tok = ctx.tok("SY")

    if "ost" in g["DBG"]:
        for t in range(NT):
            @blk.sync
            def _(sy, t=t):
                ctx.wait(sy, "SY", ctx.barrier_toks())
                ctx.dma(sy, "SY", out=g["x1stage"][:, :], in_=g["hs16" if False else "ost16"][128 * t:128 * (t + 1), :])

            @blk.vector
            def _(v, t=t):
                ctx.wait(v, "V", ctx.tok("SY"))
                ctx.inc(v.tensor_copy(out=g["lnsq"][:, :], in_=g["x1stage"][:, :]), "V")

            @blk.sync
            def _(sy, t=t):
                ctx.wait(sy, "SY", ctx.tok("V"))
                ctx.dma(sy, "SY", out=g["DBG"]["ost"][128 * t:128 * (t + 1), :], in_=g["lnsq"][:, :])

    vtile = {}
    for t in range(NT):
        pb0, pb1 = PS[2 * (t % 2)], PS[2 * (t % 2) + 1]

        @blk.sync
        def _(sy, t=t):
            if t >= 2:
                ctx.wait(sy, "SY", vtile[t - 2])
            ctx.dma(sy, "SY", out=g["resw"][t % 2][:, :], in_=g["hs"][128 * t:128 * (t + 1), :])
        res_tok = ctx.tok("SY")

        @blk.tensor
        def _(T, t=t, pb0=pb0, pb1=pb1):
            w = [wo_tok, of_tok]
            if t < 2:
                w.append(ctx.tok("V"))
            else:
                w.append(vtile[t - 2])
            ctx.wait(T, "T", w)
            for nh in range(2):
                pb = (pb0, pb1)[nh]
                for f in range(8):
                    _li = T.matmul(pb[:, 0:512],
                             g["of_fm"][f][:, 128 * t:128 * (t + 1)],
                             g["wos"][:, :].rearrange("p (kk m) -> p kk m", kk=8)[:, f, 512 * nh:512 * (nh + 1)],
                             start=(f == 0), stop=(f == 7))
            ctx.inc(_li, "T", 1)
        wo_t = ctx.tok("T")

        @blk.vector
        def _(v, t=t, pb0=pb0, pb1=pb1, wo_t=wo_t, res_tok=res_tok):
            ctx.wait(v, "V", [wo_t, res_tok])
            ctx.inc(v.tensor_tensor(out=g["resw"][t % 2][:, 0:512], in0=g["resw"][t % 2][:, 0:512],
                                    in1=pb0[:, 0:512], op=ALU.add), "V")
            ctx.inc(v.tensor_tensor(out=g["resw"][t % 2][:, 512:1024], in0=g["resw"][t % 2][:, 512:1024],
                                    in1=pb1[:, 0:512], op=ALU.add), "V")
        _ln_tile(g, g["resw"][t % 2], g["x1stage"])
        vtile[t] = ctx.tok("V")

        @blk.gpsimd
        def _(gp, t=t):
            ctx.wait(gp, "GP", vtile[t])
            ctx.dma(gp, "GP", out=g["x1s"][128 * t:128 * (t + 1), 0:D], in_=g["x1stage"][:, :])
    x1_tok = ctx.tok("GP")

    if "x1" in g["DBG"]:
        for t in range(NT):
            @blk.sync
            def _(sy, t=t):
                ctx.wait(sy, "SY", ctx.barrier_toks())
                ctx.dma(sy, "SY", out=g["x1stage"][:, :], in_=g["x1s"][128 * t:128 * (t + 1), 0:D])

            @blk.vector
            def _(v, t=t):
                ctx.wait(v, "V", ctx.tok("SY"))
                ctx.inc(v.tensor_copy(out=g["lnsq"][:, :], in_=g["x1stage"][:, :]), "V")

            @blk.sync
            def _(sy, t=t):
                ctx.wait(sy, "SY", ctx.tok("V"))
                ctx.dma(sy, "SY", out=g["DBG"]["x1"][128 * t:128 * (t + 1), :], in_=g["lnsq"][:, :])

    @blk.gpsimd
    def _(gp):
        ctx.dma(gp, "GP", out=g["cws"][:, :].rearrange("p (kk m) -> p kk m", kk=8),
                             in_=g["cwT"].rearrange("(kk p) m -> p kk m", p=128))
    cw_tok = ctx.tok("GP")

    for f in range(8):
        @blk.sync
        def _(sy, f=f):
            ctx.wait(sy, "SY", x1_tok)
            if f >= 2:
                ctx.wait(sy, "SY", ("T", ctx.cnt["T"]))
            ctx.dma(sy, "SY", out=g["x1f_str"][f % 2], in_=g["x1s"][:, 128 * f:128 * (f + 1)],
                                 transpose=True)

        @blk.tensor
        def _(T, f=f):
            ctx.wait(T, "T", [cw_tok, ctx.tok("SY")])
            if f == 0:
                ctx.wait(T, "T", ctx.tok("V"))
            for ns in range(4):
                _li = T.matmul(PS[ns][0:E, 0:512],
                         g["cws"][:, :].rearrange("p (kk m) -> p kk m", kk=8)[:, f, :],
                         g["x1f_str"][f % 2][:, 512 * ns:512 * (ns + 1)],
                         start=(f == 0), stop=(f == 7))
            ctx.inc(_li, "T", 1)

    @blk.vector
    def _(v):
        ctx.wait(v, "V", ctx.tok("T"))
        for ns in range(4):
            ctx.inc(v.tensor_copy(out=g["lgf"][0:E, 512 * ns:512 * (ns + 1)],
                                  in_=PS[ns][0:E, 0:512]), "V")

    for ns in range(4):
        @blk.tensor
        def _(T, ns=ns):
            ctx.wait(T, "T", ctx.tok("V"))
            ctx.inc(T.matmul(g["PSC"][0:1, 0:512], g["c_ones_f"][0:E, 0:1],
                             g["lgf"][0:E, 512 * ns:512 * (ns + 1)], start=True, stop=True), "T")

        @blk.vector
        def _(v, ns=ns):
            ctx.wait(v, "V", ctx.tok("T"))
            ctx.inc(v.tensor_scalar(out=g["lrowA"][0:1, 512 * ns:512 * (ns + 1)],
                                    in0=g["PSC"][0:1, 0:512], scalar1=1.0 / E, scalar2=None,
                                    op0=ALU.mult), "V")

    for ns in range(4):
        @blk.tensor
        def _(T, ns=ns):
            ctx.wait(T, "T", ctx.tok("V"))
            ctx.inc(T.matmul(g["PSC"][0:E, 0:512], g["c_onesr_f"][0:1, 0:E],
                             g["lrowA"][0:1, 512 * ns:512 * (ns + 1)], start=True, stop=True), "T")

        @blk.vector
        def _(v, ns=ns):
            ctx.wait(v, "V", ctx.tok("T"))
            sl = slice(512 * ns, 512 * (ns + 1))
            ctx.inc(v.tensor_tensor(out=g["lgf"][0:E, sl], in0=g["lgf"][0:E, sl],
                                    in1=g["PSC"][0:E, 0:512], op=ALU.subtract), "V")

    @blk.scalar
    def _(sc):
        ctx.wait(sc, "SC", ctx.tok("V"))
        ctx.inc(sc.activation(out=g["probsf"][0:E, :], in_=g["lgf"][0:E, :], func=AF.Exp), "SC")

    for ns in range(4):
        @blk.tensor
        def _(T, ns=ns):
            ctx.wait(T, "T", [ctx.tok("SC"), ctx.tok("V")])
            ctx.inc(T.matmul(g["PSC"][0:1, 0:512], g["c_ones_f"][0:E, 0:1],
                             g["probsf"][0:E, 512 * ns:512 * (ns + 1)], start=True, stop=True), "T")

        @blk.vector
        def _(v, ns=ns):
            ctx.wait(v, "V", ctx.tok("T"))
            ctx.inc(v.reciprocal(out=g["lrowB"][0:1, 512 * ns:512 * (ns + 1)],
                                 in_=g["PSC"][0:1, 0:512]), "V")

    for ns in range(4):
        @blk.tensor
        def _(T, ns=ns):
            ctx.wait(T, "T", ctx.tok("V"))
            ctx.inc(T.matmul(g["PSC"][0:E, 0:512], g["c_onesr_f"][32:33, 0:E],
                             g["lrowB"][0:1, 512 * ns:512 * (ns + 1)], start=True, stop=True), "T")

        @blk.vector
        def _(v, ns=ns):
            ctx.wait(v, "V", ctx.tok("T"))
            sl = slice(512 * ns, 512 * (ns + 1))
            ctx.inc(v.tensor_tensor(out=g["probsf"][0:E, sl], in0=g["probsf"][0:E, sl],
                                    in1=g["PSC"][0:E, 0:512], op=ALU.mult), "V")
    _dump_rows(g, "probs", g["probsf"][0:E, :])

    for t in range(NT):
        @blk.tensor
        def _(T, t=t):
            ctx.wait(T, "T", ctx.tok("V"))
            _li = T.transpose(g["PSC"][0:128, 0:E], g["probsf"][0:E, 128 * t:128 * (t + 1)],
                        g["c_identf"][0:E, 0:E])
            ctx.inc(_li, "T", 1)

        @blk.vector
        def _(v, t=t):
            ctx.wait(v, "V", ctx.tok("T"))
            ctx.inc(v.tensor_copy(out=g["probstm"][:, 8 * t:8 * (t + 1)], in_=g["PSC"][0:128, 0:E]), "V")

    @blk.vector
    def _(v):
        ctx.sw(v, "V")
        ctx.inc(v.tensor_copy(out=g["extr16"][:, :], in_=g["probstm"][:, :]), "V")

    @blk.gpsimd
    def _(gp):
        ctx.wait(gp, "GP", ctx.tok("V"))
        ctx.dma(gp, "GP", out=g["x1s"].rearrange("(t p) r -> p t r", p=128)[:, :, D:D + E],
                             in_=g["extr16"][:, :].rearrange("p (t e) -> p t e", t=NT))

    with g["nc"].named_scope("bisect"):
        _bisect(g)
    with g["nc"].named_scope("indices"):
        _indices(g, 0)


def _ln_tile(g, src_f32, dst16, par=0, vin=None, ow_free=None):
    ctx, blk = g["ctx"], g["blk"]
    b = 8 * par

    @blk.scalar
    def _(sc):
        ctx.wait(sc, "SC", vin if vin else ctx.tok("V"))
        ctx.inc(sc.activation(out=g["lnsq"][:, :], in_=src_f32[:, :], func=AF.Square,
                              accum_out=g["colw"][:, b + 3:b + 4]), "SC")
        ctx.sw(sc, "SC")
        ctx.inc(sc.activation(out=g["lnsq"][:, :], in_=src_f32[:, :], func=AF.Copy,
                              accum_out=g["colw"][:, b + 4:b + 5]), "SC")
    sc1 = ctx.tok("SC")

    @blk.vector
    def _(v):
        ctx.wait(v, "V", sc1)
        cw = g["colw"]
        ctx.inc(v.tensor_scalar(out=cw[:, b + 5:b + 6], in0=cw[:, b + 4:b + 5], scalar1=1.0 / D,
                                scalar2=None, op0=ALU.mult), "V")
        ctx.sw(v, "V")
        ctx.inc(v.tensor_tensor(out=cw[:, b + 6:b + 7], in0=cw[:, b + 5:b + 6],
                                in1=cw[:, b + 5:b + 6], op=ALU.mult), "V")
        ctx.sw(v, "V")
        ctx.inc(v.scalar_tensor_tensor(out=cw[:, b + 6:b + 7], in0=cw[:, b + 3:b + 4], scalar=1.0 / D,
                                       in1=cw[:, b + 6:b + 7], op0=ALU.mult, op1=ALU.subtract), "V")
    v1 = ctx.tok("V")

    @blk.scalar
    def _(sc):
        ctx.wait(sc, "SC", v1)
        ctx.inc(sc.activation(out=g["colw"][:, b + 7:b + 8], in_=g["colw"][:, b + 6:b + 7],
                              func=AF.Sqrt, bias=EPS_LN), "SC")
    sc2 = ctx.tok("SC")

    @blk.vector
    def _(v):
        w = [sc2] if vin else [sc2, ctx.tok("GP"), ctx.tok("SY")]
        if ow_free:
            w.append(ow_free)
        ctx.wait(v, "V", w)
        cw = g["colw"]
        ctx.inc(v.reciprocal(out=cw[:, b + 0:b + 1], in_=cw[:, b + 7:b + 8]), "V")
        ctx.sw(v, "V")
        ctx.inc(v.tensor_tensor(out=cw[:, b + 1:b + 2], in0=cw[:, b + 5:b + 6],
                                in1=cw[:, b + 0:b + 1], op=ALU.mult), "V")
        ctx.sw(v, "V")
        ctx.inc(v.tensor_scalar_mul(cw[:, b + 1:b + 2], cw[:, b + 1:b + 2], -1.0), "V")
        ctx.sw(v, "V")
        ctx.inc(v.tensor_scalar(out=dst16[:, :], in0=src_f32[:, :], scalar1=cw[:, b + 0:b + 1],
                                scalar2=cw[:, b + 1:b + 2], op0=ALU.mult, op1=ALU.add), "V")
    return ctx.tok("V")


def _bisect(g):
    ctx, blk = g["ctx"], g["blk"]
    tw = g["thrw"]

    @blk.vector
    def _(v):
        ctx.inc(v.memset(tw[0:1, 0:8], 0.0), "V")
        ctx.inc(v.memset(tw[0:1, 8:16], 1.0), "V")

    for it in range(BIS_IT + 1):
        last = (it == BIS_IT)

        @blk.vector
        def _(v, last=last):
            ctx.wait(v, "V", ctx.tok("T"))
            if not last:
                ctx.inc(v.tensor_tensor(out=tw[0:1, 16:24], in0=tw[0:1, 0:8], in1=tw[0:1, 8:16],
                                        op=ALU.add), "V")
                ctx.sw(v, "V")
                ctx.inc(v.tensor_scalar_mul(tw[0:1, 16:24], tw[0:1, 16:24], 0.5), "V")
                src = tw[0:1, 16:24]
            else:
                src = tw[0:1, 0:8]
            ctx.sw(v, "V")
            ctx.inc(v.tensor_copy(out=g["mid128"][0:1, :],
                                  in_=src.rearrange("a (b e) -> a b e", b=1).to_broadcast([1, NT, 8])), "V")

        @blk.tensor
        def _(T):
            ctx.wait(T, "T", ctx.tok("V"))
            ctx.inc(T.matmul(g["PS"][5][:, 0:128], g["c_onesr_f"][0:1, :], g["mid128"][0:1, :],
                             start=True, stop=True), "T")

        @blk.vector
        def _(v):
            ctx.wait(v, "V", ctx.tok("T"))
            ctx.inc(v.tensor_tensor(out=g["cmask"][:, :], in0=g["probstm"][:, :],
                                    in1=g["PS"][5][:, 0:128], op=ALU.is_ge), "V")
        if last:
            break

        @blk.tensor
        def _(T):
            ctx.wait(T, "T", ctx.tok("V"))
            ctx.inc(T.matmul(g["PSC"][0:1, 0:128], g["c_ones_h"][:, :], g["cmask"][:, :],
                             start=True, stop=True), "T")

        @blk.vector
        def _(v):
            ctx.wait(v, "V", ctx.tok("T"))
            ctx.inc(v.tensor_reduce(out=tw[0:1, 24:32],
                                    in_=g["PSC"][0:1, 0:128].rearrange("a (t e) -> a e t", e=8),
                                    axis=mybir.AxisListType.X, op=ALU.add), "V")
            ctx.sw(v, "V")
            ctx.inc(v.tensor_scalar(out=g["selw"][0:1, :], in0=tw[0:1, 24:32], scalar1=float(KTOP),
                                    scalar2=None, op0=ALU.is_ge), "V")
            ctx.inc(v.tensor_scalar(out=g["selnw"][0:1, :], in0=tw[0:1, 24:32], scalar1=float(KTOP),
                                    scalar2=None, op0=ALU.is_lt), "V")
            ctx.sw(v, "V")
            ctx.inc(v.copy_predicated(out=tw[0:1, 0:8], mask=g["selw"][0:1, :],
                                      data=tw[0:1, 16:24]), "V")
            ctx.inc(v.copy_predicated(out=tw[0:1, 8:16], mask=g["selnw"][0:1, :],
                                      data=tw[0:1, 16:24]), "V")
    @blk.vector
    def _(v):
        ctx.sw(v, "V")
        ctx.inc(v.tensor_copy(out=g["cmaski"][:, :], in_=g["cmask"][:, :]), "V")
    _dump_rows(g, "thr", tw[0:1, 0:8])


def _indices(g, e):
    ctx, blk = g["ctx"], g["blk"]
    for e in (e,):
        @blk.vector
        def _(v, e=e):
            ctx.wait(v, "V", ctx.tok("T"))
            ctx.inc(v.tensor_copy(out=g["cand"][:, :], in_=g["c_neg1"][:, :]), "V")
            ctx.sw(v, "V")
            ctx.inc(v.copy_predicated(out=g["cand"][:, :], mask=g["cmaski"][:, e:e + 121:8],
                                      data=g["c_iota_tok"][:, :]), "V")
            ctx.sw(v, "V")
            ctx.inc(v.max(out=g["s16"][:, 0:8], in_=g["cand"][:, :]), "V")
            ctx.sw(v, "V")
            ctx.inc(v.match_replace(out=g["cand"][:, :], in_to_replace=g["s16"][:, 0:8],
                                    in_values=g["cand"][:, :], imm_value=-1.0), "V")
            ctx.sw(v, "V")
            ctx.inc(v.max(out=g["s16"][:, 8:16], in_=g["cand"][:, :]), "V")
            ctx.sw(v, "V")
            ctx.inc(v.tensor_scalar(out=g["s16f"][:, :], in0=g["s16"][:, :], scalar1=0.0,
                                    scalar2=None, op0=ALU.max), "V")
            ctx.inc(v.tensor_reduce(out=g["kcol"][:, 0:1], in_=g["cmask"][:, e:e + 121:8],
                                    axis=mybir.AxisListType.X, op=ALU.add), "V")

        @blk.tensor
        def _(T, e=e):
            ctx.wait(T, "T", ctx.tok("V"))
            ctx.inc(T.matmul(g["PSC"][:, 0:1], g["c_maskSU"][:, :], g["kcol"][:, 0:1],
                             start=True, stop=True), "T")

        @blk.vector
        def _(v, e=e):
            ctx.wait(v, "V", ctx.tok("T"))
            ctx.inc(v.tensor_copy(out=g["basec"][:, 0:1], in_=g["PSC"][:, 0:1]), "V")
            for s in range(16):
                ctx.sw(v, "V")
                ctx.inc(v.tensor_scalar(out=g["basec"][:, 1:2], in0=g["basec"][:, 0:1],
                                        scalar1=float(s), scalar2=None, op0=ALU.add), "V")
                ctx.sw(v, "V")
                ctx.inc(v.tensor_scalar(out=g["eqw"][:, :], in0=g["c_iota_d"][:, :],
                                        scalar1=g["basec"][:, 1:2], scalar2=None,
                                        op0=ALU.is_equal), "V")
                ctx.sw(v, "V")
                if s == 0:
                    ctx.inc(v.tensor_scalar(out=g["Mplace"][:, :], in0=g["eqw"][:, :],
                                            scalar1=g["s16f"][:, 0:1], scalar2=None, op0=ALU.mult), "V")
                else:
                    ctx.inc(v.scalar_tensor_tensor(out=g["Mplace"][:, :], in0=g["eqw"][:, :],
                                                   scalar=g["s16f"][:, s:s + 1], in1=g["Mplace"][:, :],
                                                   op0=ALU.mult, op1=ALU.add), "V")

        @blk.tensor
        def _(T, e=e):
            ctx.wait(T, "T", ctx.tok("V"))
            ctx.inc(T.matmul(g["PSC"][0:1, 0:KTOP], g["c_ones_h"][:, :], g["Mplace"][:, :],
                             start=True, stop=True), "T")

        @blk.vector
        def _(v, e=e):
            ctx.wait(v, "V", ctx.tok("T"))
            ctx.inc(v.tensor_copy(out=g["idxrow"][0:1, :], in_=g["PSC"][0:1, 0:KTOP]), "V")

        @blk.tensor
        def _(T, e=e):
            ctx.wait(T, "T", ctx.tok("V"))
            for c in range(4):
                _li = T.transpose(g["PSC"][0:128, 128 + c:129 + c], g["idxrow"][0:1, 128 * c:128 * (c + 1)],
                            g["c_identf"][0:1, 0:1])
            ctx.inc(_li, "T", 1)

        @blk.vector
        def _(v, e=e):
            ctx.wait(v, "V", ctx.tok("T"))
            ctx.inc(v.tensor_copy(out=g["idxc"][:, 4 * e:4 * e + 4], in_=g["PSC"][0:128, 128:132]), "V")

    g.setdefault("idx_tok", {})[e] = ctx.tok("V")

    if e == E - 1 and "idx" in g["DBG"]:
        @blk.vector
        def _(v):
            ctx.wait(v, "V", ctx.barrier_toks())
            ctx.inc(v.tensor_copy(out=g["probstm"][:, 0:32], in_=g["idxc"][:, :]), "V")
        _dump_rows(g, "idx", g["probstm"][:, 0:32])


def _phase4(g):
    ctx, blk = g["ctx"], g["blk"]
    PS = g["PS"]
    ga_tok = [None] * (E + 1)
    tr_tok = [None] * (E + 1)
    mm1_done = [None] * (E + 1)
    sc1_done = {}
    mm_t = {}
    w1g_tok = {}
    w2_tok = {}
    scat_t = [None] * E

    txp_v = {}
    txp_done = {}
    bank_last = {0: None}

    def emit_route(e):
        par = e % 2

        @blk.gpsimd
        def _(gp, e=e, par=par):
            if e == 0:
                ctx.wait(gp, "GP", [ctx.tok("V"), ctx.tok("SY"), ctx.tok("T")])
                gp.wait_ge(g["wc_sem"], 16 * g["n_wcast"])
            tk = txp_done.get(e - 2)
            if tk:
                ctx.wait(gp, "GP", tk)
            ctx.wait(gp, "GP", g["idx_tok"][e])
            for c in range(4):
                ctx.dma(gp, "GP", indirect=True,
                    out=g["gtmP"][par][:, XR * c:XR * (c + 1)],
                    out_offset=None,
                    in_=g["x1s"][:, :],
                    in_offset=IndirectOffsetOnAxis(g["idxc"][:, 4 * e + c:4 * e + c + 1], 0))
        ga_tok[e] = ctx.tok("GP")

    def emit_txp(e, r):
        par = e % 2
        bank = g["PSB"]

        @blk.tensor
        def _(T, e=e, r=r, par=par, bank=bank):
            w = [ga_tok[e]]
            tk = bank_last[0]
            if tk:
                w.append(tk)
            if e == 0 and r == 0:
                w.append(ctx.tok("V"))
            ctx.wait(T, "T", w)
            for c in range(4):
                _li = T.transpose(bank[:, 128 * c:128 * (c + 1)],
                                  g["gtmP"][par][:, XR * c + 128 * r:XR * c + 128 * (r + 1)],
                                  g["c_identb"][:, :])
            ctx.inc(_li, "T", 1)
        tt = ctx.tok("T")
        if r == 8:
            txp_done[e] = tt

        @blk.vector
        def _(v, e=e, r=r, par=par, bank=bank, tt=tt):
            ctx.wait(v, "V", tt)
            ctx.inc(v.tensor_copy(out=g["xe_fmP"][par][:, KTOP * r:KTOP * (r + 1)],
                                  in_=bank[:, 0:512]), "V")
        txp_v[(e, r)] = bank_last[0] = ctx.tok("V")

    def emit_gcol(e):
        par = e % 2

        @blk.sync
        def _(sy, e=e, par=par):
            ctx.wait(sy, "SY", txp_v[(e, 8)])
            ctx.dma(sy, "SY", out=g["growP"][par][0:1, :],
                                 in_=g["xe_fmP"][par][e:e + 1, KTOP * 8:KTOP * 9])
        grow_t = ctx.tok("SY")

        @blk.tensor
        def _(T, e=e, par=par, grow_t=grow_t):
            w = [grow_t]
            if bank_last[0]:
                w.append(bank_last[0])
            ctx.wait(T, "T", w)
            for c in range(4):
                _li = T.transpose(g["PSB"][0:128, 2 * c:2 * c + 1],
                                  g["growP"][par][0:1, 128 * c:128 * (c + 1)],
                                  g["c_identb"][0:1, 0:1])
            ctx.inc(_li, "T", 1)
        gc_t = ctx.tok("T")

        @blk.vector
        def _(v, e=e, gc_t=gc_t):
            ctx.wait(v, "V", gc_t)
            ctx.inc(v.tensor_copy(out=g["gcol"][:, 4 * e:4 * e + 4], in_=g["PSB"][0:128, 0:7:2]), "V")
        bank_last[0] = ctx.tok("V")

    def emit_w1g(e, grp, prev_tok):
        @blk.sync
        def _(sy, e=e, grp=grp, prev_tok=prev_tok):
            if prev_tok:
                ctx.wait(sy, "SY", prev_tok)
            for mm in range(4 * grp, 4 * grp + 4):
                ctx.dma(sy, "SY",
                    out=g["w1s2"][mm % 8][:, :].rearrange("p (kk q) -> p kk q", kk=8),
                    in_=g["w116"][e][:, 128 * mm:128 * (mm + 1)].rearrange("(kk p) q -> p kk q", p=128))
        w1g_tok[(e, grp)] = ctx.tok("SY")

    def emit_w2(e):
        @blk.sync
        def _(sy, e=e):
            tk = mm_t.get((e - 1, "mm2"))
            if tk:
                ctx.wait(sy, "SY", tk)
            for kk2 in range(16):
                ctx.dma(sy, "SY", out=g["w2s"][kk2][:, :],
                                     in_=g["w216"][e][128 * kk2:128 * (kk2 + 1), :])
        w2_tok[e] = ctx.tok("SY")

    emit_route(0)
    for r in range(9):
        emit_txp(0, r)
    emit_gcol(0)
    emit_w1g(0, 0, None)
    emit_w1g(0, 1, None)

    for e in range(E):
        par = e % 2
        for m in range(16):
            if m == 0 and e + 1 < E:
                _indices(g, e + 1)
                emit_route(e + 1)
            if m == 2:
                emit_w2(e)
            if m >= 6 and m % 2 == 0 and e + 1 < E:
                emit_txp(e + 1, (m - 6) // 2)
            @blk.tensor
            def _(T, e=e, m=m, par=par):
                ctx.wait(T, "T", [txp_v[(e, 7)], w1g_tok[(e, m // 4)]])
                tk = sc1_done.get((e, m - 4))
                if tk:
                    ctx.wait(T, "T", tk)
                elif m < 4:
                    tk2 = mm_t.get((e - 1, "mm2sc"))
                    if tk2:
                        ctx.wait(T, "T", tk2)
                pb = PS[m % 4]
                for kk in range(8):
                    _li = T.matmul(pb[:, 0:512],
                             g["w1s2"][m % 8][:, :].rearrange("p (kk q) -> p kk q", kk=8)[:, kk, :],
                             g["xe_fmP"][par][:, KTOP * kk:KTOP * (kk + 1)], start=(kk == 0), stop=(kk == 7))
                ctx.inc(_li, "T", 1)
            mm_t[(e, m)] = ctx.tok("T")
            if m % 4 == 3:
                gnext = m // 4 + 2
                if gnext < 4:
                    emit_w1g(e, gnext, mm_t[(e, m)])
                elif e + 1 < E:
                    emit_w1g(e + 1, gnext - 4, mm_t[(e, m)])

            @blk.scalar
            def _(sc, e=e, m=m):
                ctx.wait(sc, "SC", mm_t[(e, m)])
                ctx.inc(sc.activation(out=g["h_fm"][m], in_=PS[m % 4][:, 0:512], func=AF.Silu), "SC")
            sc1_done[(e, m)] = ctx.tok("SC")
        mm1_done[e] = mm_t[(e, 15)]

        y_sc = {}
        mmt = None
        for tm_ in range(4):
            if e + 1 < E:
                emit_txp(e + 1, 5 + tm_)
                if tm_ == 3:
                    emit_gcol(e + 1)
            @blk.tensor
            def _(T, e=e, tm_=tm_):
                ctx.wait(T, "T", [sc1_done[(e, 15)], w2_tok[e]])
                tk = y_sc.get(tm_ - 1)
                if tk:
                    ctx.wait(T, "T", tk)
                elif tm_ == 0:
                    tk2 = mm_t.get((e - 1, "mm2sc"))
                    if tk2:
                        ctx.wait(T, "T", tk2)
                    elif e == 0:
                        ctx.wait(T, "T", ctx.tok("V"))
                for kk2 in range(16):
                    for nh in range(2):
                        _li = T.matmul(PS[4 + nh][:, 0:512],
                                 g["h_fm"][kk2][:, 128 * tm_:128 * (tm_ + 1)],
                                 g["w2s"][kk2][:, 512 * nh:512 * (nh + 1)],
                                 start=(kk2 == 0), stop=(kk2 == 15))
                ctx.inc(_li, "T", 1)
            mmt = ctx.tok("T")

            @blk.vector
            def _(v, e=e, tm_=tm_, mmt=mmt):
                ctx.wait(v, "V", mmt)
                if e > 0 and scat_t[e - 1]:
                    ctx.wait(v, "V", scat_t[e - 1])
                for nh in range(2):
                    ctx.inc(v.tensor_scalar(
                        out=g["yscat"][:, 1024 * tm_ + 512 * nh:1024 * tm_ + 512 * (nh + 1)],
                        in0=PS[4 + nh][:, 0:512],
                        scalar1=g["gcol"][:, 4 * e + tm_:4 * e + tm_ + 1],
                        scalar2=None, op0=ALU.mult), "V")
            y_sc[tm_] = ctx.tok("V")
        mm_t[(e, "mm2")] = mmt
        mm_t[(e, "mm2sc")] = y_sc[3]

        @blk.gpsimd
        def _(gp, e=e, y3=y_sc[3]):
            ctx.wait(gp, "GP", y3)
            for c in range(4):
                ctx.dma(gp, "GP", indirect=True, 
                    out=g["accd"][:, :],
                    out_offset=IndirectOffsetOnAxis(g["idxc"][:, 4 * e + c:4 * e + c + 1], 0),
                    in_=g["yscat"][:, 1024 * c:1024 * (c + 1)], in_offset=None,
                    compute_op=ALU.add)
        scat_t[e] = ctx.tok("GP")
    g["scat_tok"] = ctx.tok("GP")

    if "acc" in g["DBG"]:
        for t in range(NT):
            @blk.sync
            def _(sy, t=t):
                ctx.wait(sy, "SY", ctx.barrier_toks())
                ctx.dma(sy, "SY", out=g["DBG"]["acc"][128 * t:128 * (t + 1), :],
                                     in_=g["accd"][128 * t:128 * (t + 1), :])


def _phase5(g):
    ctx, blk = g["ctx"], g["blk"]
    fin_v = {}
    add_v = {}
    out_sy = {}
    for t in range(NT):
        par = t % 2

        @blk.sync
        def _(sy, t=t, par=par):
            ctx.wait(sy, "SY", g["scat_tok"])
            if t >= 2:
                ctx.wait(sy, "SY", fin_v[t - 2])
            ctx.dma(sy, "SY", out=g["resw"][par][:, :], in_=g["accd"][128 * t:128 * (t + 1), :])
        acc_sy = ctx.tok("SY")

        @blk.gpsimd
        def _(gp, t=t, par=par):
            if t >= 2:
                ctx.wait(gp, "GP", add_v[t - 2])
            ctx.dma(gp, "GP", out=g["x1stP"][par][:, :], in_=g["x1s"][128 * t:128 * (t + 1), 0:D])
        x1_gp = ctx.tok("GP")

        @blk.vector
        def _(v, t=t, par=par, acc_sy=acc_sy, x1_gp=x1_gp):
            ctx.wait(v, "V", [acc_sy, x1_gp])
            ctx.inc(v.tensor_tensor(out=g["resw"][par][:, :], in0=g["resw"][par][:, :],
                                    in1=g["x1stP"][par][:, :], op=ALU.add), "V")
        add_v[t] = ctx.tok("V")
        fin_v[t] = _ln_tile(g, g["resw"][par], g["outwP"][par], par=par,
                            vin=add_v[t], ow_free=out_sy.get(t - 2))

        @blk.sync
        def _(sy, t=t, par=par):
            ctx.wait(sy, "SY", fin_v[t])
            ctx.dma(sy, "SY", out=g["out"][128 * t:128 * (t + 1), :], in_=g["outwP"][par][:, :])
        out_sy[t] = ctx.tok("SY")

    @blk.sync
    def _(sy):
        sy.wait_ge(ctx.sem["SY"], ctx.cnt["SY"])


def host_in_map(inputs, core):
    f32 = np.float32
    hs = np.ascontiguousarray(np.asarray(inputs["L0_attn_hidden_states"], f32)[core])
    wq = np.asarray(inputs["L0_attn_Wq"], f32)
    wk = np.asarray(inputs["L0_attn_Wk"], f32)
    wv = np.asarray(inputs["L0_attn_Wv"], f32)
    wb = np.asarray(inputs["L0_attn_Wb"], f32)
    wdec = np.asarray(inputs["L0_attn_Wdec"], f32)
    wmix = np.asarray(inputs["L0_attn_Wmix"], f32)
    ws = np.zeros((16, D), f32)
    ws[0:4] = wb
    ws[4:8] = wdec
    ws[8:12] = wmix
    convw = np.concatenate([np.asarray(inputs["L0_attn_wq_conv"], f32),
                            np.asarray(inputs["L0_attn_wk_conv"], f32),
                            np.asarray(inputs["L0_attn_wv_conv"], f32)], axis=1)
    return {
        "hs": hs,
        "wqT": np.ascontiguousarray(wq.T),
        "wkT": np.ascontiguousarray(wk.T),
        "wvT": np.ascontiguousarray(wv.T),
        "wsT": np.ascontiguousarray(ws.T),
        "mixb": np.asarray(inputs["L0_attn_mix_bias"], f32).reshape(1, 4),
        "convw": np.ascontiguousarray(convw),
        "onw": np.asarray(inputs["L0_attn_onorm_w"], f32).reshape(1, DK),
        "woT": np.ascontiguousarray(np.asarray(inputs["L0_attn_Wo"], f32).T),
        "cwT": np.ascontiguousarray(np.asarray(inputs["L0_ffn_choice_w"], f32).T),
        "w1": np.ascontiguousarray(np.asarray(inputs["L0_ffn_w1"], f32)),
        "w2": np.ascontiguousarray(np.asarray(inputs["L0_ffn_w2"], f32)),
    }


_BUILD_CACHE = {}


def kernel(_dbg=(), _upto="p5", **inputs):
    key = (tuple(sorted(_dbg)), _upto)
    if key not in _BUILD_CACHE:
        _BUILD_CACHE[key] = build(_dbg, _upto)
    nc, DBG = _BUILD_CACHE[key]
    in_maps = [host_in_map(inputs, core) for core in range(8)]
    res = run_bass_kernel_spmd(nc, in_maps, core_ids=list(range(8)))
    out = np.stack([r["out"] for r in res.results], axis=0)
    dbgout = {k: [r["dbg_" + k] for r in res.results] for k in DBG}
    return (out, dbgout) if (_dbg or _upto != "p5") else out


# version 19
# speedup vs baseline: 1.0090x; 1.0090x over previous
import contextlib
import numpy as np
import ml_dtypes
import concourse.bass as bass
import concourse.mybir as mybir
from concourse.bass import IndirectOffsetOnAxis
from concourse.bass_utils import run_bass_kernel_spmd

F32 = mybir.dt.float32
BF16 = mybir.dt.bfloat16
F16 = mybir.dt.float16
I32 = mybir.dt.int32
AF = mybir.ActivationFunctionType
ALU = mybir.AluOpType

S, D, H, DK = 2048, 1024, 4, 256
NCH, NT, E, KTOP, FF = 16, 16, 8, 512, 2048
XR = 1152
EPS_L2, EPS_RMS, EPS_LN = 1e-6, 1e-5, 1e-5
BIS_IT = 45
ENGS = ("T", "V", "SC", "SY", "GP")


class Ctx:
    def __init__(self, nc):
        self.nc = nc
        self.sem = {}
        self.cnt = dict.fromkeys(ENGS, 0)
        self.wmark = {}
        self.lastwait = {"SY": 0, "GP": 0}
        self.chained = {"SY": 0, "GP": 0}

    def inc(self, instr, eng, d=None):
        if d is None:
            d = 16 if eng in ("SY", "GP") else 1
        instr.then_inc(self.sem[eng], d)
        self.cnt[eng] += d
        return (eng, self.cnt[eng])

    def tok(self, eng):
        if eng in ("SY", "GP"):
            self.lastwait[eng] = max(self.lastwait[eng], self.cnt[eng])
        return (eng, self.cnt[eng])

    def sw(self, eng_obj, name):
        key = (name, name)
        if self.cnt[name] > self.wmark.get(key, 0):
            eng_obj.wait_ge(self.sem[name], self.cnt[name])
            self.wmark[key] = self.cnt[name]

    def dma(self, eng_obj, q, indirect=False, **kw):
        lw = self.lastwait[q]
        if self.chained[q] < lw:
            eng_obj.wait_ge(self.sem[q], lw)
            self.chained[q] = lw
        instr = eng_obj.indirect_dma_start(**kw) if indirect else eng_obj.dma_start(**kw)
        return self.inc(instr, q)

    def wait(self, eng_obj, waiter, tokens):
        if tokens and isinstance(tokens, tuple) and isinstance(tokens[0], str):
            tokens = [tokens]
        skey = (waiter, waiter)
        if self.cnt[waiter] > self.wmark.get(skey, 0):
            eng_obj.wait_ge(self.sem[waiter], self.cnt[waiter])
            self.wmark[skey] = self.cnt[waiter]
        for (e, v) in tokens:
            if e in ("SY", "GP"):
                self.lastwait[e] = max(self.lastwait[e], v)
            if v <= 0:
                continue
            if e == waiter and e not in ("SY", "GP"):
                continue
            key = (waiter, e)
            if self.wmark.get(key, 0) >= v:
                continue
            eng_obj.wait_ge(self.sem[e], v)
            self.wmark[key] = v

    def barrier_toks(self):
        return [(e, self.cnt[e]) for e in ENGS]


class Arena:
    def __init__(self, ar, n):
        self.ar, self.n, self.placed = ar, n, []

    def alloc(self, name, elems, phases, dt=BF16):
        sz = ((elems if dt == BF16 else 2 * elems) + 1) // 2 * 2
        cands = sorted([(o, s) for (_, o, s, p) in self.placed if p & set(phases)])
        off = 0
        for (o, s) in cands:
            if off + sz <= o:
                break
            off = max(off, o + s)
        assert off + sz <= self.n, (name, off, sz, self.n)
        self.placed.append((name, off, sz, set(phases)))
        if dt == BF16:
            return self.ar[:, off:off + elems]
        return self.ar.bitcast(F32)[:, off // 2:off // 2 + elems]


def build(dbg=(), upto="p5"):
    dbg = set(dbg)
    nc = bass.Bass("TRN2", target_bir_lowering=False, debug=False)
    g = {"nc": nc, "dbgset": dbg}

    def dram(name, shp, dt, kind):
        g[name] = nc.dram_tensor(name, shp, dt, kind=kind).ap()
    dram("hs", [S, D], F32, "ExternalInput")
    dram("wqT", [D, D], F32, "ExternalInput")
    dram("wkT", [D, D], F32, "ExternalInput")
    dram("wvT", [D, D], F32, "ExternalInput")
    dram("wsT", [D, 16], F32, "ExternalInput")
    dram("mixb", [1, 4], F32, "ExternalInput")
    dram("convw", [D, 12], F32, "ExternalInput")
    dram("onw", [1, DK], F32, "ExternalInput")
    dram("woT", [D, D], F32, "ExternalInput")
    dram("cwT", [D, E], F32, "ExternalInput")
    dram("w1", [E, D, FF], F32, "ExternalInput")
    dram("w2", [E, FF, D], F32, "ExternalInput")
    dram("out", [S, D], F32, "ExternalOutput")
    dram("hs16", [S, D], BF16, "Internal")
    dram("sc16", [3 * DK, S], BF16, "Internal")
    dram("ost16", [S, D], BF16, "Internal")
    dram("x1s", [S, XR], BF16, "Internal")
    dram("xes0", [KTOP, XR], BF16, "Internal")
    dram("xes1", [KTOP, XR], BF16, "Internal")
    dram("accd", [S, D], F32, "Internal")
    dram("w116", [E, D, FF], BF16, "Internal")
    dram("w216", [E, FF, D], BF16, "Internal")

    DBG = {}
    for name, shp in (("v", [256, S]), ("qn", [256, S]), ("kn", [256, S]),
                      ("bgm", [16, S]), ("ema", [256, S]), ("x1", [S, D]),
                      ("probs", [E, S]), ("thr", [1, 8]), ("idx", [128, 32]),
                      ("acc", [S, D]), ("tt0", [128, 128]), ("ost", [S, D])):
        if name in dbg:
            DBG[name] = nc.dram_tensor("dbg_" + name, shp, F32, kind="ExternalOutput").ap()
    g["DBG"] = DBG

    ii, jj = np.mgrid[0:128, 0:128]
    consts = [
        ("c_identb", np.eye(128, dtype=ml_dtypes.bfloat16)),
        ("c_identf", np.eye(128, dtype=np.float32)),
        ("c_maskSL", (ii > jj).astype(np.float32)),
        ("c_maskSU", (ii < jj).astype(np.float32)),
        ("c_maskSLn", -(ii > jj).astype(np.float32)),
        ("c_maskSUn", -(ii < jj).astype(np.float32)),
        ("c_maskIU", (ii <= jj).astype(np.float32)),
        ("c_ones_b", np.ones((128, 1), dtype=ml_dtypes.bfloat16)),
        ("c_ones_f", np.ones((128, 1), dtype=np.float32)),
        ("c_ones_h", np.ones((128, 1), dtype=np.float16)),
        ("c_onesr_f", np.ones((33, 128), dtype=np.float32)),
        ("c_iota_d", np.tile(np.arange(KTOP, dtype=np.float16)[None, :], (128, 1))),
        ("c_iota_tok", (np.arange(16)[None, :] * 128 + np.arange(128)[:, None]).astype(np.float16)),
        ("c_neg1", np.full((128, 16), -1.0, dtype=np.float16)),
        ("c_eps", np.tile(np.array([[EPS_L2, EPS_RMS, EPS_LN, 0.0]], dtype=np.float32), (128, 1))),
    ]
    g["cdram"] = {n: nc.inline_tensor(v, n + "_d").ap() for n, v in consts}
    g["cnames"] = [n for n, _ in consts]

    st = contextlib.ExitStack()
    sbuf = lambda n, shp, dt: st.enter_context(nc.sbuf_tensor(n, shp, dt))
    psum = lambda n, shp, dt: st.enter_context(nc.psum_tensor(n, shp, dt))

    ARN = 90112
    arT = sbuf("AR", [128, ARN], BF16)
    g["AR"] = arT
    ar = Arena(arT, ARN)
    hF = {"h"}
    g["hs_fm8"] = ar.alloc("hs_fm", 16384, {"p0", "p1"} | hF)
    g["hs_fm"] = [g["hs_fm8"][:, 2048 * f:2048 * (f + 1)] for f in range(8)]
    g["qf"] = ar.alloc("qf", 4096, hF)
    g["kf"] = ar.alloc("kf", 4096, hF)
    ktm_a = ar.alloc("ktm", 4096, hF)
    vtm_a = ar.alloc("vtm", 4096, hF)
    emtm_a = ar.alloc("emtm", 4096, hF)
    g["ktm"] = [ktm_a[:, 256 * t:256 * (t + 1)] for t in range(NT)]
    g["vtm"] = [vtm_a[:, 256 * t:256 * (t + 1)] for t in range(NT)]
    g["emtm"] = [emtm_a[:, 256 * t:256 * (t + 1)] for t in range(NT)]
    ttst_a = ar.alloc("ttst", 2048, hF)
    g["ttst"] = [ttst_a[:, 128 * c:128 * (c + 1)] for c in range(NCH)]
    g["vf"] = ar.alloc("vf", 4096, hF)
    g["emaf16"] = g["vf"]
    g["qpreP"] = [ar.alloc(f"qpre{i}", 2064, hF) for i in range(2)]
    g["diagP"] = [ar.alloc(f"diag{i}", 512, hF) for i in range(2)]
    g["convaccP"] = [ar.alloc(f"convacc{i}", 2048, hF) for i in range(2)]
    g["sqw"] = ar.alloc("sqw", 2048, hF)
    cwk = ar.alloc("cwk", 2048, hF)
    g["cwkP"] = [cwk[:, 128 * i:128 * (i + 1)] for i in range(6)]
    g["cwkQ"] = [cwk[:, 768 + 128 * i:768 + 128 * (i + 1)] for i in range(5)]
    g["cwkN"] = cwk[:, 1408:1536]
    g["cwkNT"] = cwk[:, 1536:1664]
    g["cwkTT"] = [cwk[:, 1664 + 128 * i:1664 + 128 * (i + 1)] for i in range(2)]
    g["cwkA"] = ar.alloc("cwkA", 128, hF)
    cwk2 = ar.alloc("cwk2", 2048, hF)
    g["cwkP2"] = [cwk2[:, 128 * i:128 * (i + 1)] for i in range(6)]
    g["cwkQ2"] = [cwk2[:, 768 + 128 * i:768 + 128 * (i + 1)] for i in range(5)]
    g["cwkN2"] = cwk2[:, 1408:1536]
    g["cwkNT2"] = cwk2[:, 1536:1664]
    g["cwkTT2"] = [cwk2[:, 1664 + 128 * i:1664 + 128 * (i + 1)] for i in range(2)]
    g["cwkA2"] = ar.alloc("cwkA2", 128, hF)
    g["sS"] = ar.alloc("sS", 512, hF)
    g["sf32"] = ar.alloc("sf32", 512, hF, F32)
    g["scw_r"] = ar.alloc("scw_r", 256, hF)
    g["scw_u"] = ar.alloc("scw_u", 256, hF)
    g["sqscr"] = ar.alloc("sqscr", 256, hF)
    g["ssqc"] = ar.alloc("ssqc", 48, hF, F32)
    g["nbcol"] = ar.alloc("nbcol", 16, hF, F32)
    ostc_a = ar.alloc("ostc", 4096, hF)
    g["ostc_all"] = ostc_a
    g["ostc"] = [ostc_a[:, 256 * t:256 * (t + 1)] for t in range(NT)]
    g["wqsP"] = [ar.alloc(f"wqs{i}", 2048, hF) for i in range(2)]
    g["wksP"] = [ar.alloc(f"wks{i}", 2048, hF) for i in range(2)]
    g["wvsP"] = [ar.alloc(f"wvs{i}", 2048, hF) for i in range(2)]
    g["gb_h"] = ar.alloc("gb_h", S, hF, F32)
    g["gd1"] = ar.alloc("gd1", S, hF, F32)
    lrow2 = ar.alloc("lrow", S, hF | {"p1", "p3"}, F32)
    g["lrowA"] = lrow2[0:1, 0:S]
    g["lrowB"] = lrow2[32:33, 0:S]

    g["of_fm8"] = ar.alloc("of_fm", 16384, {"p3"})
    g["of_fm"] = [g["of_fm8"][:, 2048 * f:2048 * (f + 1)] for f in range(8)]
    g["wos"] = ar.alloc("wos", 8192, {"p3"})
    g["x1f_str"] = [ar.alloc(f"x1f{i}", 2048, {"p3"}) for i in range(2)]
    g["lgf"] = ar.alloc("lgf", S, {"p3"}, F32)
    g["probsf"] = ar.alloc("probsf", S, {"p3"}, F32)
    g["lnsq"] = ar.alloc("lnsq", D, {"p3", "p5"}, F32)
    g["x1stage"] = ar.alloc("x1stage", 1024, {"p3", "p5"})
    g["ws_stage"] = ar.alloc("ws_stage", 128, {"p1"})
    g["cws"] = ar.alloc("cws", 64, {"p3"})

    h_fm_a = ar.alloc("h_fm", 8192, {"p4"})
    g["h_fm"] = [h_fm_a[:, 512 * m:512 * (m + 1)] for m in range(16)]
    g["xe_fmP"] = [ar.alloc(f"xe_fm{i}", 9 * KTOP, {"p4"}) for i in range(2)]
    g["gtmP"] = [ar.alloc(f"gtm{i}", 4 * XR, {"p4"}) for i in range(2)]
    g["growP"] = [ar.alloc(f"grow{i}", KTOP, {"p4"}) for i in range(2)]
    g["w1s2"] = [ar.alloc(f"w1s{i}", 1024, {"p4"}) for i in range(8)]
    w2all = ar.alloc("w2all", 16384, {"p4"})
    g["w2s"] = [w2all[:, 1024 * i:1024 * (i + 1)] for i in range(16)]
    g["yscat"] = ar.alloc("yscat", 4 * D, {"p4"}, F32)

    g["stg16"] = [ar.alloc(f"stg16_{i}", 1024, {"p0"}) for i in range(2)]
    g["zz16"] = ar.alloc("zz16", 2 * D, {"p0"})
    g["resw"] = [ar.alloc(f"resw{i}", D, {"p0", "p3", "p5"}, F32) for i in range(2)]
    g["outw"] = ar.alloc("outw", D, {"p5"}, F32)
    g["outwP"] = [g["outw"], ar.alloc("outw2", D, {"p5"}, F32)]
    g["x1stP"] = [ar.alloc(f"x1st{i}", 1024, {"p3", "p5"}) for i in range(2)]

    for n, shp, dt in (
        ("bgm", [16, S], F32), ("bgm16", [16, S], BF16), ("mixtm", [128, NT * 24], F32),
        ("onwb", [128, DK], BF16), ("c_maskSL", [128, 128], F32), ("c_maskSU", [128, 128], F32),
        ("c_maskSLn", [128, 128], F32), ("c_maskSUn", [128, 128], F32),
        ("c_maskIU", [128, 128], F32), ("c_identb", [128, 128], BF16), ("c_identf", [128, 128], F32),
        ("c_ones_b", [128, 1], BF16), ("c_ones_f", [128, 1], F32), ("c_ones_h", [128, 1], F16),
        ("c_onesr_f", [33, 128], F32),
        ("c_iota_d", [128, KTOP], F16), ("c_iota_tok", [128, 16], F16), ("c_neg1", [128, 16], F16),
        ("mixbias", [16, 1], F32), ("cwv", [128, 96], F32), ("probstm", [128, 128], F32),
        ("extr16", [128, 128], BF16), ("thrw", [1, 48], F32), ("mid128", [1, 128], F32),
        ("cmask", [128, 128], F16), ("cmaski", [128, 128], mybir.dt.uint8),
        ("selw", [1, 8], I32), ("selnw", [1, 8], I32), ("cand", [128, 16], F16), ("s16", [128, 16], F16),
        ("s16f", [128, 16], F32), ("kcol", [128, 1], F32), ("basec", [128, 2], F32),
        ("eqw", [128, KTOP], F16), ("Mplace", [128, KTOP], F16), ("idxrow", [1, KTOP], F32),
        ("idxc", [128, 4 * E], I32), ("gcol", [128, 4 * E], F32), ("grow", [1, KTOP], BF16),
        ("omix", [128, DK], F32), ("otmp", [128, DK], F32), ("colw", [128, 16], F32),
        ("c_eps", [128, 4], F32),
    ):
        g[n] = sbuf(n, shp, dt)

    g["PS"] = [psum(f"ps{i}", [128, 512], F32) for i in range(6)]
    g["PSB"] = psum("psb", [128, 512], BF16)
    g["PSC"] = psum("psc", [128, 512], F32)

    nc.const_aps.register(F32, EPS_L2, g["c_eps"][:, 0:1])
    nc.const_aps.register(F32, EPS_RMS, g["c_eps"][:, 1:2])
    nc.const_aps.register(F32, EPS_LN, g["c_eps"][:, 2:3])
    ctx = Ctx(nc)
    g["ctx"] = ctx
    g["wqkv_tok"] = {}
    g["xesd"] = [g["xes0"], g["xes1"]]
    with st:
        for e in ENGS:
            ctx.sem[e] = st.enter_context(nc.semaphore("sem" + e))
        g["wc_sem"] = st.enter_context(nc.semaphore("semWC"))
        with nc.Block() as blk:
            g["blk"] = blk
            seq = ["p0", "p1", "h0", "h1", "h2", "h3", "p3", "p4", "p5"]
            stop = seq.index(upto)
            with nc.named_scope("p0"):
                _phase0(g)
            if stop >= 1:
                with nc.named_scope("p1"):
                    _phase1(g)
            for h in range(H):
                if stop >= 2 + h:
                    with nc.named_scope(f"head{h}"):
                        _head(g, h)
            if stop >= 6:
                with nc.named_scope("p3"):
                    _phase3(g)
            if stop >= 7:
                with nc.named_scope("p4"):
                    _phase4(g)
            if stop >= 8:
                with nc.named_scope("p5"):
                    _phase5(g)
            ctx2 = g["ctx"]
            @blk.sync
            def _(sy):
                for e in ENGS:
                    if e != "SY" and ctx2.cnt[e] > 0:
                        sy.wait_ge(ctx2.sem[e], ctx2.cnt[e])
                sy.wait_ge(ctx2.sem["SY"], ctx2.cnt["SY"])
    return nc, DBG


def _dump_rows(g, name, src_f32):
    if name not in g["DBG"]:
        return
    ctx, blk = g["ctx"], g["blk"]
    dst = g["DBG"][name]

    @blk.sync
    def _(sy):
        ctx.wait(sy, "SY", ctx.barrier_toks())
        ctx.dma(sy, "SY", out=dst, in_=src_f32)


def _dump16(g, name, src_bf16_2ct):
    if name not in g["DBG"]:
        return
    ctx, blk = g["ctx"], g["blk"]
    dst = g["DBG"][name]
    for ct in range(2):
        @blk.vector
        def _(v, ct=ct):
            ctx.wait(v, "V", ctx.barrier_toks())
            ctx.inc(v.tensor_copy(out=g["gd1"][:, :],
                                  in_=src_bf16_2ct[:, 2048 * ct:2048 * (ct + 1)]), "V")

        @blk.sync
        def _(sy, ct=ct):
            ctx.wait(sy, "SY", ctx.tok("V"))
            ctx.dma(sy, "SY", out=dst[128 * ct:128 * (ct + 1), :], in_=g["gd1"][:, :])


def _phase0(g):
    ctx, blk = g["ctx"], g["blk"]

    @blk.sync
    def _(sy):
        for nm in g["cnames"]:
            ctx.dma(sy, "SY", out=g[nm][:, :], in_=g["cdram"][nm][:, :])
        ctx.dma(sy, "SY", out=g["cwv"][:, :].rearrange("p (hc w) -> p hc w", hc=8),
                in_=g["convw"].rearrange("(hc p) w -> p hc w", p=128))

    @blk.vector
    def _(v):
        ctx.inc(v.memset(g["zz16"][:, :], 0.0), "V")
        ctx.inc(v.memset(g["idxc"][:, :], 0), "V")

    for t in range(NT):
        @blk.sync
        def _(sy, t=t):
            if t >= 2:
                ctx.wait(sy, "SY", ctx.tok("GP"))
            ctx.dma(sy, "SY", out=g["resw"][t % 2][:, :], in_=g["hs"][128 * t:128 * (t + 1), :])

        @blk.vector
        def _(v, t=t):
            ctx.wait(v, "V", ctx.tok("SY"))
            ctx.inc(v.tensor_copy(out=g["stg16"][t % 2][:, :], in_=g["resw"][t % 2][:, :]), "V")

        @blk.gpsimd
        def _(gp, t=t):
            ctx.wait(gp, "GP", ctx.tok("V"))
            ctx.dma(gp, "GP", out=g["hs16"][128 * t:128 * (t + 1), :], in_=g["stg16"][t % 2][:, :])

    @blk.gpsimd
    def _(gp):
        zzf = g["zz16"].bitcast(F32)
        for i in range(NT):
            ctx.dma(gp, "GP", out=g["accd"].rearrange("(a p) d -> a p d", p=128)[i],
                                 in_=zzf[:, 0:1024])
        ctx.wait(gp, "GP", ctx.tok("V"))
        ctx.dma(gp, "GP", out=g["x1s"].rearrange("(t p) r -> p t r", p=128)[:, :, D + E:XR],
                in_=g["zz16"][:, 0:NT * (XR - D - E)].rearrange("p (t r) -> p t r", t=NT))

    @blk.sync
    def _(sy):
        ctx.wait(sy, "SY", ctx.tok("GP"))
        for f in range(8):
            ctx.dma(sy, "SY", out=g["hs_fm"][f], in_=g["hs16"][:, 128 * f:128 * (f + 1)],
                                 transpose=True)
    g["hsfm_tok"] = ctx.tok("SY")
    g["cw_tok0"] = ("SY", 16 * len(g["cnames"]) + 16)


def _phase1(g):
    ctx, blk = g["ctx"], g["blk"]
    PS = g["PS"]

    @blk.vector
    def _(v):
        ctx.inc(v.memset(g["mixbias"][:, :], 0.0), "V")

    @blk.sync
    def _(sy):
        ctx.wait(sy, "SY", ctx.tok("V"))
        ctx.dma(sy, "SY", out=g["mixbias"][8:12, 0:1],
                             in_=g["mixb"][0:1, :].rearrange("a (b c) -> (a b) c", c=1))

    @blk.gpsimd
    def _(gp):
        ctx.dma(gp, "GP", out=g["ws_stage"][:, :].rearrange("p (kk m) -> p kk m", kk=8),
                             in_=g["wsT"].rearrange("(kk p) m -> p kk m", p=128))
    pre = [ctx.tok("GP"), ctx.tok("SY"), g["hsfm_tok"]]

    for nh in range(4):
        @blk.tensor
        def _(T, nh=nh):
            ctx.wait(T, "T", pre + [ctx.tok("SC")])
            pb = PS[nh % 2]
            for f in range(8):
                _li = T.matmul(pb[0:16, 0:512],
                         g["ws_stage"][:, :].rearrange("p (kk m) -> p kk m", kk=8)[:, f, :],
                         g["hs_fm"][f][:, 512 * nh:512 * (nh + 1)], start=(f == 0), stop=(f == 7))
            ctx.inc(_li, "T", 1)

        @blk.scalar
        def _(sc, nh=nh):
            ctx.wait(sc, "SC", ctx.tok("T"))
            ctx.inc(sc.activation(out=g["bgm"][:, 512 * nh:512 * (nh + 1)],
                                  in_=PS[nh % 2][0:16, 0:512], func=AF.Sigmoid,
                                  bias=g["mixbias"][:, 0:1]), "SC")

    @blk.vector
    def _(v):
        ctx.wait(v, "V", ctx.tok("SC"))
        ctx.inc(v.tensor_copy(out=g["bgm16"][:, :], in_=g["bgm"][:, :]), "V")

    for t in range(NT):
        @blk.tensor
        def _(T, t=t):
            ctx.wait(T, "T", ctx.tok("V"))
            _li = T.transpose(g["PSB"][0:128, 0:16], g["bgm16"][0:16, 128 * t:128 * (t + 1)],
                        g["c_identb"][0:16, 0:16])
            ctx.inc(_li, "T", 1)

        @blk.vector
        def _(v, t=t):
            ctx.wait(v, "V", ctx.tok("T"))
            ctx.inc(v.tensor_copy(out=g["mixtm"][:, 24 * t:24 * t + 16], in_=g["PSB"][0:128, 0:16]), "V")

    _dump_rows(g, "bgm", g["bgm"][:, :])

    @blk.sync
    def _(sy):
        ctx.dma(sy, "SY", out=g["lrowA"][0:1, 0:DK], in_=g["onw"][:, :])

    @blk.tensor
    def _(T):
        ctx.wait(T, "T", [ctx.tok("SY"), ctx.tok("V")])
        ctx.inc(T.matmul(g["PSC"][:, 0:DK], g["c_onesr_f"][0:1, :], g["lrowA"][0:1, 0:DK],
                         start=True, stop=True), "T")

    @blk.vector
    def _(v):
        ctx.wait(v, "V", ctx.tok("T"))
        ctx.inc(v.tensor_copy(out=g["onwb"][:, :], in_=g["PSC"][:, 0:DK]), "V")
    g["psb_tok"] = ctx.tok("V")


def _wcast_chunk(g, h):
    ctx, blk = g["ctx"], g["blk"]

    @blk.gpsimd
    def _(gp):
        if h + 1 < H:
            ctx.wait(gp, "GP", g["proj_mm_tok"])
            par = (h + 1) % 2
            for wd, wl in (("wqT", "wqsP"), ("wkT", "wksP"), ("wvT", "wvsP")):
                ctx.dma(gp, "GP",
                    out=g[wl][par][:, :].rearrange("p (kk m) -> p kk m", kk=8),
                    in_=g[wd][:, 256 * (h + 1):256 * (h + 2)].rearrange("(kk p) m -> p kk m", p=128))
        g["wqkv_tok"][h + 1] = ctx.tok("GP")
        for e in (2 * h, 2 * h + 1):
            gp.dma_start(out=g["w116"][e][:, :], in_=g["w1"][e][:, :]).then_inc(g["wc_sem"], 16)
            gp.dma_start(out=g["w216"][e][:, :], in_=g["w2"][e][:, :]).then_inc(g["wc_sem"], 16)
    g["n_wcast"] = g.get("n_wcast", 0) + 4


def _head(g, h):
    nc = g["nc"]
    with nc.named_scope(f"proj{h}"):
        _head_proj(g, h)
    with nc.named_scope(f"ema{h}"):
        _head_ema(g, h)
    _wcast_chunk(g, h)
    with nc.named_scope(f"tph{h}"):
        _head_tphase(g, h)
        _head_vtmscale(g, h)
    with nc.named_scope(f"scan{h}"):
        _head_scan(g, h)
    _head_ost_write(g, h)


def _head_vtmscale(g, h):
    ctx, blk = g["ctx"], g["blk"]

    @blk.vector
    def _(v):
        for c in range(NCH):
            ctx.inc(v.tensor_scalar(out=g["vtm"][c], in0=g["vtm"][c],
                                    scalar1=g["mixtm"][:, 24 * c + h:24 * c + h + 1],
                                    scalar2=None, op0=ALU.mult), "V")


def _head_tmx(g, h, kind, cpair):
    ctx, blk = g["ctx"], g["blk"]
    src, dst, stok = {
        "k": (g["kf"], g["ktm"], g.get("l2_tok")),
        "v": (g["vf"], g["vtm"], g.get("_last_silu")),
        "em": (g["emaf16"], g["emtm"], g.get("ema_tok")),
    }[kind]
    c0 = 2 * cpair

    @blk.tensor
    def _(T, src=src, stok=stok, c0=c0):
        w = []
        if g.get("psb_tok"):
            w.append(g["psb_tok"])
        if stok:
            w.append(stok)
        ctx.wait(T, "T", w)
        for j, (c, ct) in enumerate(((c0, 0), (c0, 1), (c0 + 1, 0), (c0 + 1, 1))):
            _li = T.transpose(g["PSB"][:, 128 * j:128 * (j + 1)],
                              src[:, 2048 * ct + 128 * c:2048 * ct + 128 * (c + 1)],
                              g["c_identb"][:, :])
        ctx.inc(_li, "T", 1)
    tt = ctx.tok("T")

    @blk.vector
    def _(v, dst=dst, c0=c0, tt=tt):
        ctx.wait(v, "V", tt)
        ctx.inc(v.tensor_copy(out=dst[c0], in_=g["PSB"][:, 0:256]), "V")
        ctx.inc(v.tensor_copy(out=dst[c0 + 1], in_=g["PSB"][:, 256:512]), "V")
    g["psb_tok"] = ctx.tok("V")


def _head_proj(g, h):
    ctx, blk = g["ctx"], g["blk"]
    PS = g["PS"]
    par_h = h % 2
    specs = [("wqT", g["wqsP"][par_h], g["qf"], 0, True),
             ("wkT", g["wksP"][par_h], g["kf"], 1, True),
             ("wvT", g["wvsP"][par_h], g["vf"], 2, False)]

    @blk.vector
    def _(v):
        if h == 0:
            ctx.wait(v, "V", ctx.barrier_toks())
        ctx.inc(v.memset(g["qpreP"][0][:, 0:8], 0.0), "V")
        ctx.inc(v.memset(g["qpreP"][1][:, 0:8], 0.0), "V")


    if h == 0:
        @blk.gpsimd
        def _(gp):
            ctx.wait(gp, "GP", ctx.barrier_toks())
            for wd, wsl, _, _, _ in specs:
                ctx.dma(gp, "GP",
                    out=wsl[:, :].rearrange("p (kk m) -> p kk m", kk=8),
                    in_=g[wd][:, 0:256].rearrange("(kk p) m -> p kk m", p=128))
        wtok = ctx.tok("GP")
    else:
        wtok = g["wqkv_tok"][h]

    cwtok = g.get("cw_tok0") or ctx.tok("SY")

    prev_evac = {}
    conv_done = {}
    diag_v = {}
    silu_sc = {}
    ev_prev = {}
    prev_silu = g.get("_last_silu")
    pend = {"st": None}

    def emit_conv_step():
        st = pend["st"]
        if st is None or st["nh"] >= 4:
            return
        idx, par, ct, dst, nh = st["idx"], st["par"], st["ct"], st["dst"], st["nh"]

        @blk.tensor
        def _(T, idx=idx, par=par, nh=nh):
            w = [diag_v[idx], prev_evac[idx]]
            if nh >= 2:
                w.append(silu_sc[(idx, nh - 2)])
            elif (idx - 1, nh + 2) in silu_sc:
                w.append(silu_sc[(idx - 1, nh + 2)])
            elif prev_silu:
                w.append(prev_silu)
            ctx.wait(T, "T", w)
            pb = PS[4 + nh % 2]
            for j in range(4):
                _li = T.matmul(pb[:, 0:512], g["diagP"][par][:, 128 * j:128 * (j + 1)],
                         g["qpreP"][par][:, 8 - (3 - j) + 512 * nh:8 - (3 - j) + 512 * nh + 512],
                         start=(j == 0), stop=(j == 3))
            ctx.inc(_li, "T", 1)
        cmt = ctx.tok("T")

        @blk.scalar
        def _(sc, idx=idx, par=par, ct=ct, dst=dst, nh=nh, cmt=cmt):
            ctx.wait(sc, "SC", cmt)
            ctx.inc(sc.activation(out=dst[:, 2048 * ct + 512 * nh:2048 * ct + 512 * (nh + 1)],
                                  in_=PS[4 + nh % 2][:, 0:512], func=AF.Silu), "SC")
        silu_sc[(idx, nh)] = ctx.tok("SC")
        st["nh"] += 1
        if st["nh"] >= 4:
            conv_done[idx] = cmt
            g["_last_silu"] = silu_sc[(idx, 3)]

    idx = 0
    for wd, wsl, dst, ci, do_norm in specs:
        for ct in range(2):
            par = idx % 2
            hc = 2 * h + ct
            @blk.vector
            def _(v, ci=ci, hc=hc, par=par, idx=idx):
                ctx.wait(v, "V", cwtok)
                tk = conv_done.get(idx - 2)
                if tk:
                    ctx.wait(v, "V", tk)
                for j in range(4):
                    ctx.inc(v.tensor_scalar(out=g["diagP"][par][:, 128 * j:128 * (j + 1)],
                                            in0=g["c_identb"][:, :],
                                            scalar1=g["cwv"][:, 12 * hc + 4 * ci + j:12 * hc + 4 * ci + j + 1],
                                            scalar2=None, op0=ALU.mult), "V")
            diag_v[idx] = ctx.tok("V")

            ev = {}
            for nh in range(4):
                @blk.tensor
                def _(T, ct=ct, nh=nh, wsl=wsl, idx=idx, ev=ev):
                    w = [wtok]
                    tk2 = ev_prev.get((idx - 1, nh))
                    if tk2:
                        w.append(tk2)
                    if idx == 0:
                        if h == 0 and nh == 0:
                            w.append(ctx.tok("V"))
                        elif nh >= 2:
                            w.append(diag_v[0])
                    ctx.wait(T, "T", w)
                    pb = PS[nh]
                    for kk in range(8):
                        _li = T.matmul(pb[:, 0:512],
                                 wsl[:, :].rearrange("p (kk m) -> p kk m", kk=8)[:, kk, 128 * ct:128 * (ct + 1)],
                                 g["hs_fm"][kk][:, 512 * nh:512 * (nh + 1)],
                                 start=(kk == 0), stop=(kk == 7))
                    ctx.inc(_li, "T", 1)
                nh_t = ctx.tok("T")

                @blk.vector
                def _(v, nh=nh, nh_t=nh_t, par=par, idx=idx):
                    ctx.wait(v, "V", nh_t)
                    if nh == 0:
                        tk = conv_done.get(idx - 2)
                        if tk:
                            ctx.wait(v, "V", tk)
                    ctx.inc(v.tensor_copy(out=g["qpreP"][par][:, 8 + 512 * nh:8 + 512 * (nh + 1)],
                                          in_=PS[nh][:, 0:512]), "V")
                ev[nh] = ev_prev[(idx, nh)] = ctx.tok("V")
                emit_conv_step()
            prev_evac[idx] = ev[3]
            pend["st"] = {"idx": idx, "par": par, "ct": ct, "dst": dst, "nh": 0}
            idx += 1
    for _ in range(4):
        emit_conv_step()

    g["proj_mm_tok"] = ctx.tok("T")

    stok = _l2norm(g, g["qf"], prev_evac[5], None)
    stok = _l2norm(g, g["kf"], prev_evac[5], stok)
    g["l2_tok"] = stok

    if h == 0:
        _dump16(g, "v", g["vf"])
        _dump16(g, "qn", g["qf"])
        _dump16(g, "kn", g["kf"])


def _l2norm(g, dst, vcover, prev_scale):
    ctx, blk = g["ctx"], g["blk"]
    PS = g["PS"]
    sum_t = {}
    for ct in range(2):
        @blk.scalar
        def _(sc, ct=ct):
            if ct > 0:
                ctx.wait(sc, "SC", sum_t[ct - 1])
            ctx.inc(sc.activation(out=g["sqw"][:, :], in_=dst[:, 2048 * ct:2048 * (ct + 1)],
                                  func=AF.Square), "SC")
        sq_t = ctx.tok("SC")

        @blk.tensor
        def _(T, ct=ct, sq_t=sq_t):
            w = [sq_t]
            if ct == 0:
                w.append(vcover)
                if prev_scale:
                    w.append(prev_scale)
            ctx.wait(T, "T", w)
            for nh in range(4):
                _li = T.matmul(PS[2 + nh][0:1, 0:512], g["c_ones_b"][:, :],
                         g["sqw"][:, 512 * nh:512 * (nh + 1)], start=(ct == 0), stop=(ct == 1))
            ctx.inc(_li, "T", 1)
        sum_t[ct] = ctx.tok("T")

    @blk.scalar
    def _(sc):
        ctx.wait(sc, "SC", sum_t[1])
        for nh in range(4):
            ctx.inc(sc.activation(out=g["lrowA"][0:1, 512 * nh:512 * (nh + 1)],
                                  in_=PS[2 + nh][0:1, 0:512], func=AF.Sqrt, bias=EPS_L2), "SC")
    sqrt_t = ctx.tok("SC")

    @blk.vector
    def _(v):
        ctx.wait(v, "V", sqrt_t)
        ctx.inc(v.reciprocal(out=g["lrowB"][0:1, :], in_=g["lrowA"][0:1, :]), "V")
    rtok = ctx.tok("V")

    @blk.tensor
    def _(T):
        ctx.wait(T, "T", rtok)
        for nh in range(4):
            _li = T.matmul(PS[2 + nh][:, 0:512], g["c_onesr_f"][32:33, :],
                     g["lrowB"][0:1, 512 * nh:512 * (nh + 1)], start=True, stop=True)
        ctx.inc(_li, "T", 1)
    bc_t = ctx.tok("T")

    @blk.vector
    def _(v):
        ctx.wait(v, "V", bc_t)
        for ct in range(2):
            for nh in range(4):
                sl = dst[:, 2048 * ct + 512 * nh:2048 * ct + 512 * (nh + 1)]
                ctx.inc(v.tensor_tensor(out=sl, in0=sl, in1=PS[2 + nh][:, 0:512], op=ALU.mult), "V")
    return ctx.tok("V")


def _head_vbounce(g, h):
    ctx, blk = g["ctx"], g["blk"]

    @blk.gpsimd
    def _(gp):
        ctx.wait(gp, "GP", ctx.tok("SC"))
        ctx.dma(gp, "GP", out=g["sc16"][DK:2 * DK, :].rearrange("(ct p) s -> p ct s", p=128),
                             in_=g["vf"][:, :].rearrange("p (ct s) -> p ct s", ct=2))
    g["vb_tok"] = ctx.tok("GP")


def _head_ema(g, h):
    ctx, blk = g["ctx"], g["blk"]
    PS = g["PS"]
    for cp in range(NCH // 2):
        _head_tmx(g, h, "v", cp)

    @blk.sync
    def _(sy):
        ctx.wait(sy, "SY", [ctx.tok("V"), ctx.tok("T")])
        ctx.dma(sy, "SY", out=g["lrowA"][0:1, :], in_=g["bgm"][4 + h:5 + h, :])
        ctx.dma(sy, "SY", out=g["lrowB"][0:1, :], in_=g["bgm"][0 + h:1 + h, :])
    g["brow_tok"] = ctx.tok("SY")

    @blk.tensor
    def _(T):
        ctx.wait(T, "T", [ctx.tok("V"), ctx.tok("SC"), g["brow_tok"]])
        for nh in range(4):
            _li = T.matmul(PS[2 + nh][:, 0:512], g["c_onesr_f"][0:1, :],
                     g["lrowA"][0:1, 512 * nh:512 * (nh + 1)], start=True, stop=True)
        ctx.inc(_li, "T", 1)

    @blk.vector
    def _(v):
        ctx.wait(v, "V", ctx.tok("T"))
        for nh in range(4):
            ctx.inc(v.tensor_copy(out=g["gb_h"][:, 512 * nh:512 * (nh + 1)], in_=PS[2 + nh][:, 0:512]), "V")

    for ct in range(2):
        @blk.vector
        def _(v, ct=ct):
            ctx.sw(v, "V")
            ctx.inc(v.tensor_scalar(out=g["gd1"][:, :], in0=g["gb_h"][:, :], scalar1=-1.0,
                                    scalar2=1.0, op0=ALU.mult, op1=ALU.add), "V")
            ctx.sw(v, "V")
            ctx.inc(v.tensor_tensor(out=g["gd1"][:, :], in0=g["gd1"][:, :],
                                    in1=g["vf"][:, 2048 * ct:2048 * (ct + 1)], op=ALU.mult), "V")
            ctx.sw(v, "V")
            ctx.inc(v.tensor_tensor_scan(out=g["emaf16"][:, 2048 * ct:2048 * (ct + 1)],
                                         data0=g["gb_h"][:, :], data1=g["gd1"][:, :],
                                         initial=0.0, op0=ALU.mult, op1=ALU.add), "V")
    g["ema_tok"] = ctx.tok("V")
    if h == 0:
        _dump16(g, "ema", g["emaf16"])


def _head_tphase(g, h):
    ctx, blk = g["ctx"], g["blk"]
    PS = g["PS"]
    vt = {}
    tt = {}

    def emitT(c, step, fn):
        @blk.tensor
        def _(T, c=c, step=step):
            if step == 0:
                if c >= 2 and (c - 2, 13) in vt:
                    ctx.wait(T, "T", vt[(c - 2, 13)])
                else:
                    ctx.wait(T, "T", ctx.tok("V"))
            else:
                ctx.wait(T, "T", vt[(c, step - 1)])
            fn(T)
        tt[(c, step)] = ctx.tok("T")

    def emitV(c, step, fn):
        @blk.vector
        def _(v, c=c, step=step):
            ctx.wait(v, "V", tt[(c, step)])
            ctx.sw(v, "V")
            fn(v, c)
        vt[(c, step)] = ctx.tok("V")

    W = [{"P": g["cwkP"], "Q": g["cwkQ"], "N": g["cwkN"], "NT": g["cwkNT"],
          "TT": g["cwkTT"], "A": g["cwkA"]},
         {"P": g["cwkP2"], "Q": g["cwkQ2"], "N": g["cwkN2"], "NT": g["cwkNT2"],
          "TT": g["cwkTT2"], "A": g["cwkA2"]}]

    def chunk_steps(c):
        pb = PS[c % 2]
        w = W[c % 2]
        steps = []

        def s0(T, c=c, pb=pb):
            for kt in range(2):
                _li = T.matmul(pb[:, 0:128],
                         g["kf"][:, 2048 * kt + 128 * c:2048 * kt + 128 * (c + 1)],
                         g["kf"][:, 2048 * kt + 128 * c:2048 * kt + 128 * (c + 1)],
                         start=(kt == 0), stop=(kt == 1))
            _li = T.matmul(pb[:, 128:256], g["c_onesr_f"][32:33, :],
                     g["lrowB"][0:1, 128 * c:128 * (c + 1)], start=True, stop=True)
            ctx.inc(_li, "T", 1)
        steps.append(("T", s0))

        def v0(v, c, pb=pb, w=w):
            ctx.inc(v.scalar_tensor_tensor(out=w["N"], in0=pb[:, 0:128],
                                           scalar=g["mixtm"][:, 24 * c + h:24 * c + h + 1],
                                           in1=g["c_maskSLn"][:, :], op0=ALU.mult, op1=ALU.mult), "V")
            ctx.inc(v.tensor_tensor(out=w["A"], in0=pb[:, 0:128], in1=g["c_maskSUn"][:, :],
                                    op=ALU.mult), "V")
            ctx.sw(v, "V")
            ctx.inc(v.tensor_tensor(out=w["NT"], in0=w["A"], in1=pb[:, 128:256], op=ALU.mult), "V")
            ctx.sw(v, "V")
            ctx.inc(v.tensor_tensor(out=w["TT"][0], in0=w["NT"], in1=g["c_identb"][:, :],
                                    op=ALU.add), "V")
        steps.append(("V", v0))

        tt_par = [0]
        Pp = [w["N"], w["NT"]]
        for j in range(1, 7):
            def tj(T, c=c, pb=pb, j=j, P=Pp[0], Q=Pp[1]):
                _li = T.matmul(pb[:, 256:384], Q, P, start=True, stop=True)
                if j <= 5:
                    _li = T.matmul(pb[:, 384:512], P, Q, start=True, stop=True)
                ctx.inc(_li, "T", 1)
            steps.append(("T", tj))

            def vj(v, c, pb=pb, w=w, j=j):
                ctx.inc(v.tensor_copy(out=w["P"][j - 1], in_=pb[:, 256:384]), "V")
                if j <= 5:
                    ctx.inc(v.tensor_copy(out=w["Q"][j - 1], in_=pb[:, 384:512]), "V")
            steps.append(("V", vj))

            def tj2(T, c=c, pb=pb, w=w, j=j, par=tt_par[0]):
                ctx.inc(T.matmul(pb[:, 256:384], w["P"][j - 1], w["TT"][par],
                                 start=True, stop=True), "T")
            steps.append(("T", tj2))

            def vj2(v, c, pb=pb, w=w, j=j, par=tt_par[0]):
                dst = g["ttst"][c] if j == 6 else w["TT"][1 - par]
                ctx.inc(v.tensor_tensor(out=dst, in0=w["TT"][par], in1=pb[:, 256:384],
                                        op=ALU.add), "V")
            steps.append(("V", vj2))
            Pp = [w["P"][j - 1], (w["Q"][j - 1] if j <= 5 else None)]
            tt_par[0] = 1 - tt_par[0]
        return steps

    for cbase in range(0, NCH, 2):
        _head_tmx(g, h, "k", cbase // 2)
        sa = chunk_steps(cbase)
        sb = chunk_steps(cbase + 1)
        ia = ib = 0
        ka = kb = 0
        while ia < len(sa) or ib < len(sb):
            if ia == 12 and sa[ia][0] == "T":
                _head_tmx(g, h, "em", cbase // 2)
            if ia < len(sa):
                kind, fn = sa[ia]
                (emitT if kind == "T" else emitV)(cbase, ka if kind == "T" else ka, fn)
                if kind == "V":
                    ka += 1
                ia += 1
            if ib < len(sb):
                kind, fn = sb[ib]
                (emitT if kind == "T" else emitV)(cbase + 1, kb if kind == "T" else kb, fn)
                if kind == "V":
                    kb += 1
                ib += 1

    if h == 0 and "tt0" in g["DBG"]:
        @blk.vector
        def _(v):
            ctx.wait(v, "V", ctx.barrier_toks())
            ctx.inc(v.tensor_copy(out=g["gd1"][:, 0:128], in_=g["ttst"][0]), "V")
        _dump_rows(g, "tt0", g["gd1"][:, 0:128])


def _head_scan(g, h):
    ctx, blk = g["ctx"], g["blk"]
    PS = g["PS"]
    v3_tok = {}

    @blk.vector
    def _(v):
        for c in range(NCH):
            ctx.inc(v.tensor_scalar_mul(g["nbcol"][:, c:c + 1],
                                        g["mixtm"][:, 24 * c + h:24 * c + h + 1], -1.0), "V")

    for c in range(NCH):
        @blk.tensor
        def _(T, c=c):
            ctx.wait(T, "T", v3_tok[c - 1] if c > 0 else ctx.tok("V"))
            _li = None
            if c > 0:
                for kt in range(2):
                    _li = T.matmul(PS[3][:, 0:256],
                             g["kf"][:, 2048 * kt + 128 * c:2048 * kt + 128 * (c + 1)],
                             g["sS"][:, 256 * kt:256 * (kt + 1)], start=(kt == 0), stop=(kt == 1))
            ctx.inc(_li if _li is not None else T.nop(), "T", 1)

        @blk.vector
        def _(v, c=c):
            ctx.wait(v, "V", ctx.tok("T"))
            if c == 0 and g.get("ost_tok"):
                ctx.wait(v, "V", g["ost_tok"])
            if c > 0:
                ctx.inc(v.scalar_tensor_tensor(
                    out=g["scw_r"], in0=PS[3][:, 0:256],
                    scalar=g["nbcol"][:, c:c + 1],
                    in1=g["vtm"][c], op0=ALU.mult, op1=ALU.add), "V")
            else:
                ctx.inc(v.tensor_copy(out=g["scw_r"], in_=g["vtm"][c]), "V")

        @blk.tensor
        def _(T, c=c):
            ctx.wait(T, "T", ctx.tok("V"))
            _li = T.matmul(PS[3][:, 256:512], g["ttst"][c], g["scw_r"], start=True, stop=True)
            for kt in range(2):
                _li = T.matmul(PS[4][:, 0:128],
                         g["kf"][:, 2048 * kt + 128 * c:2048 * kt + 128 * (c + 1)],
                         g["qf"][:, 2048 * kt + 128 * c:2048 * kt + 128 * (c + 1)],
                         start=(kt == 0), stop=(kt == 1))
            ctx.inc(_li, "T", 1)

        @blk.vector
        def _(v, c=c):
            ctx.wait(v, "V", ctx.tok("T"))
            ctx.inc(v.tensor_copy(out=g["scw_u"], in_=PS[3][:, 256:512]), "V")
            ctx.inc(v.tensor_tensor(out=g["cwkA"], in0=PS[4][:, 0:128], in1=g["c_maskIU"][:, :],
                                    op=ALU.mult), "V")

        @blk.tensor
        def _(T, c=c):
            ctx.wait(T, "T", ctx.tok("V"))
            first = True
            if c > 0:
                for kt in range(2):
                    _li = T.matmul(PS[4][:, 128:384],
                             g["qf"][:, 2048 * kt + 128 * c:2048 * kt + 128 * (c + 1)],
                             g["sS"][:, 256 * kt:256 * (kt + 1)], start=(kt == 0), stop=False)
                first = False
            _li = T.matmul(PS[4][:, 128:384], g["cwkA"], g["scw_u"], start=first, stop=True)
            for kt in range(2):
                _li = T.matmul((PS[2], PS[5])[kt][:, 0:256], g["ktm"][c][:, 128 * kt:128 * (kt + 1)],
                         g["scw_u"], start=True, stop=True)
            ctx.inc(_li, "T", 1)

        @blk.vector
        def _(v, c=c):
            ctx.wait(v, "V", ctx.tok("T"))
            if c == 0:
                ctx.inc(v.tensor_copy(out=g["sf32"][:, 0:256], in_=PS[2][:, 0:256]), "V")
                ctx.inc(v.tensor_copy(out=g["sf32"][:, 256:512], in_=PS[5][:, 0:256]), "V")
            else:
                ctx.inc(v.tensor_tensor(out=g["sf32"][:, 0:256], in0=g["sf32"][:, 0:256],
                                        in1=PS[2][:, 0:256], op=ALU.add), "V")
                ctx.inc(v.tensor_tensor(out=g["sf32"][:, 256:512], in0=g["sf32"][:, 256:512],
                                        in1=PS[5][:, 0:256], op=ALU.add), "V")
            ctx.sw(v, "V")
            ctx.inc(v.tensor_copy(out=g["sS"][:, :], in_=g["sf32"][:, :]), "V")
        v3_tok[c] = ctx.tok("V")

        @blk.vector
        def _(v, c=c):
            ctx.sw(v, "V")
            ctx.inc(v.tensor_tensor(out=g["otmp"][:, :], in0=g["emtm"][c], in1=PS[4][:, 128:384],
                                    op=ALU.subtract), "V")
            ctx.sw(v, "V")
            ctx.inc(v.scalar_tensor_tensor(
                out=g["omix"][:, :], in0=g["otmp"][:, :],
                scalar=g["mixtm"][:, 24 * c + 8 + h:24 * c + 9 + h],
                in1=PS[4][:, 128:384], op0=ALU.mult, op1=ALU.add), "V")
            ctx.sw(v, "V")
            ctx.inc(v.tensor_copy(out=g["ostc"][c], in_=g["omix"][:, :]), "V")
        omix_tok = ctx.tok("V")

        @blk.scalar
        def _(sc, c=c, ost_v=omix_tok):
            ctx.wait(sc, "SC", ost_v)
            ctx.inc(sc.activation(out=g["sqscr"], in_=g["ostc"][c], func=AF.Square,
                                  accum_out=g["ssqc"][:, c:c + 1]), "SC")
    sq_all = ctx.tok("SC")

    @blk.scalar
    def _(sc):
        ctx.wait(sc, "SC", sq_all)
        ctx.inc(sc.activation(out=g["ssqc"][:, 16:32], in_=g["ssqc"][:, 0:16],
                              func=AF.Sqrt, scale=1.0 / DK, bias=EPS_RMS), "SC")
    rms_sc = ctx.tok("SC")

    @blk.vector
    def _(v):
        ctx.wait(v, "V", rms_sc)
        ctx.inc(v.reciprocal(out=g["ssqc"][:, 32:48], in_=g["ssqc"][:, 16:32]), "V")
        ctx.sw(v, "V")
        for c in range(NCH):
            ctx.inc(v.scalar_tensor_tensor(
                out=g["ostc"][c], in0=g["ostc"][c], scalar=g["ssqc"][:, 32 + c:33 + c],
                in1=g["onwb"][:, :], op0=ALU.mult, op1=ALU.mult), "V")


def _head_ost_write(g, h):
    ctx, blk = g["ctx"], g["blk"]

    @blk.sync
    def _(sy):
        ctx.wait(sy, "SY", ctx.tok("V"))
        ctx.dma(sy, "SY", 
            out=g["ost16"].rearrange("(t p) d -> p t d", p=128)[:, :, 256 * h:256 * (h + 1)],
            in_=g["ostc_all"][:, :].rearrange("p (t d) -> p t d", t=NT))
    g["ost_tok"] = ctx.tok("SY")


def _phase3(g):
    ctx, blk = g["ctx"], g["blk"]
    PS = g["PS"]

    @blk.gpsimd
    def _(gp):
        ctx.wait(gp, "GP", ctx.barrier_toks())
        ctx.dma(gp, "GP", out=g["wos"][:, :].rearrange("p (kk m) -> p kk m", kk=8),
                             in_=g["woT"].rearrange("(kk p) m -> p kk m", p=128))
    wo_tok = ctx.tok("GP")

    @blk.sync
    def _(sy):
        ctx.wait(sy, "SY", [g["ost_tok"], ctx.tok("GP")])
        for f in range(8):
            ctx.dma(sy, "SY", out=g["of_fm"][f], in_=g["ost16"][:, 128 * f:128 * (f + 1)],
                                 transpose=True)
    of_tok = ctx.tok("SY")

    if "ost" in g["DBG"]:
        for t in range(NT):
            @blk.sync
            def _(sy, t=t):
                ctx.wait(sy, "SY", ctx.barrier_toks())
                ctx.dma(sy, "SY", out=g["x1stage"][:, :], in_=g["hs16" if False else "ost16"][128 * t:128 * (t + 1), :])

            @blk.vector
            def _(v, t=t):
                ctx.wait(v, "V", ctx.tok("SY"))
                ctx.inc(v.tensor_copy(out=g["lnsq"][:, :], in_=g["x1stage"][:, :]), "V")

            @blk.sync
            def _(sy, t=t):
                ctx.wait(sy, "SY", ctx.tok("V"))
                ctx.dma(sy, "SY", out=g["DBG"]["ost"][128 * t:128 * (t + 1), :], in_=g["lnsq"][:, :])

    vtile = {}
    gp_x1 = {}
    for t in range(NT):
        pb0, pb1 = PS[2 * (t % 2)], PS[2 * (t % 2) + 1]

        @blk.sync
        def _(sy, t=t):
            if t >= 2:
                ctx.wait(sy, "SY", vtile[t - 2])
            ctx.dma(sy, "SY", out=g["resw"][t % 2][:, :], in_=g["hs"][128 * t:128 * (t + 1), :])
        res_tok = ctx.tok("SY")

        @blk.tensor
        def _(T, t=t, pb0=pb0, pb1=pb1):
            w = [wo_tok, of_tok]
            if t < 2:
                w.append(ctx.tok("V"))
            else:
                w.append(vtile[t - 2])
            ctx.wait(T, "T", w)
            for nh in range(2):
                pb = (pb0, pb1)[nh]
                for f in range(8):
                    _li = T.matmul(pb[:, 0:512],
                             g["of_fm"][f][:, 128 * t:128 * (t + 1)],
                             g["wos"][:, :].rearrange("p (kk m) -> p kk m", kk=8)[:, f, 512 * nh:512 * (nh + 1)],
                             start=(f == 0), stop=(f == 7))
            ctx.inc(_li, "T", 1)
        wo_t = ctx.tok("T")

        @blk.vector
        def _(v, t=t, pb0=pb0, pb1=pb1, wo_t=wo_t, res_tok=res_tok):
            ctx.wait(v, "V", [wo_t, res_tok])
            ctx.inc(v.tensor_tensor(out=g["resw"][t % 2][:, 0:512], in0=g["resw"][t % 2][:, 0:512],
                                    in1=pb0[:, 0:512], op=ALU.add), "V")
            ctx.inc(v.tensor_tensor(out=g["resw"][t % 2][:, 512:1024], in0=g["resw"][t % 2][:, 512:1024],
                                    in1=pb1[:, 0:512], op=ALU.add), "V")
        addv = ctx.tok("V")
        vtile[t] = _ln_tile(g, g["resw"][t % 2], g["x1stP"][t % 2], par=t % 2,
                            vin=addv, ow_free=gp_x1.get(t - 2))

        @blk.gpsimd
        def _(gp, t=t):
            ctx.wait(gp, "GP", vtile[t])
            ctx.dma(gp, "GP", out=g["x1s"][128 * t:128 * (t + 1), 0:D], in_=g["x1stP"][t % 2][:, :])
        gp_x1[t] = ctx.tok("GP")
    x1_tok = ctx.tok("GP")

    if "x1" in g["DBG"]:
        for t in range(NT):
            @blk.sync
            def _(sy, t=t):
                ctx.wait(sy, "SY", ctx.barrier_toks())
                ctx.dma(sy, "SY", out=g["x1stage"][:, :], in_=g["x1s"][128 * t:128 * (t + 1), 0:D])

            @blk.vector
            def _(v, t=t):
                ctx.wait(v, "V", ctx.tok("SY"))
                ctx.inc(v.tensor_copy(out=g["lnsq"][:, :], in_=g["x1stage"][:, :]), "V")

            @blk.sync
            def _(sy, t=t):
                ctx.wait(sy, "SY", ctx.tok("V"))
                ctx.dma(sy, "SY", out=g["DBG"]["x1"][128 * t:128 * (t + 1), :], in_=g["lnsq"][:, :])

    @blk.gpsimd
    def _(gp):
        ctx.dma(gp, "GP", out=g["cws"][:, :].rearrange("p (kk m) -> p kk m", kk=8),
                             in_=g["cwT"].rearrange("(kk p) m -> p kk m", p=128))
    cw_tok = ctx.tok("GP")

    for f in range(8):
        @blk.sync
        def _(sy, f=f):
            ctx.wait(sy, "SY", x1_tok)
            if f >= 2:
                ctx.wait(sy, "SY", ("T", ctx.cnt["T"]))
            ctx.dma(sy, "SY", out=g["x1f_str"][f % 2], in_=g["x1s"][:, 128 * f:128 * (f + 1)],
                                 transpose=True)

        @blk.tensor
        def _(T, f=f):
            ctx.wait(T, "T", [cw_tok, ctx.tok("SY")])
            if f == 0:
                ctx.wait(T, "T", ctx.tok("V"))
            for ns in range(4):
                _li = T.matmul(PS[ns][0:E, 0:512],
                         g["cws"][:, :].rearrange("p (kk m) -> p kk m", kk=8)[:, f, :],
                         g["x1f_str"][f % 2][:, 512 * ns:512 * (ns + 1)],
                         start=(f == 0), stop=(f == 7))
            ctx.inc(_li, "T", 1)

    @blk.vector
    def _(v):
        ctx.wait(v, "V", ctx.tok("T"))
        for ns in range(4):
            ctx.inc(v.tensor_copy(out=g["lgf"][0:E, 512 * ns:512 * (ns + 1)],
                                  in_=PS[ns][0:E, 0:512]), "V")

    for ns in range(4):
        @blk.tensor
        def _(T, ns=ns):
            ctx.wait(T, "T", ctx.tok("V"))
            ctx.inc(T.matmul(g["PSC"][0:1, 0:512], g["c_ones_f"][0:E, 0:1],
                             g["lgf"][0:E, 512 * ns:512 * (ns + 1)], start=True, stop=True), "T")

        @blk.vector
        def _(v, ns=ns):
            ctx.wait(v, "V", ctx.tok("T"))
            ctx.inc(v.tensor_scalar(out=g["lrowA"][0:1, 512 * ns:512 * (ns + 1)],
                                    in0=g["PSC"][0:1, 0:512], scalar1=1.0 / E, scalar2=None,
                                    op0=ALU.mult), "V")

    for ns in range(4):
        @blk.tensor
        def _(T, ns=ns):
            ctx.wait(T, "T", ctx.tok("V"))
            ctx.inc(T.matmul(g["PSC"][0:E, 0:512], g["c_onesr_f"][0:1, 0:E],
                             g["lrowA"][0:1, 512 * ns:512 * (ns + 1)], start=True, stop=True), "T")

        @blk.vector
        def _(v, ns=ns):
            ctx.wait(v, "V", ctx.tok("T"))
            sl = slice(512 * ns, 512 * (ns + 1))
            ctx.inc(v.tensor_tensor(out=g["lgf"][0:E, sl], in0=g["lgf"][0:E, sl],
                                    in1=g["PSC"][0:E, 0:512], op=ALU.subtract), "V")

    @blk.scalar
    def _(sc):
        ctx.wait(sc, "SC", ctx.tok("V"))
        ctx.inc(sc.activation(out=g["probsf"][0:E, :], in_=g["lgf"][0:E, :], func=AF.Exp), "SC")

    for ns in range(4):
        @blk.tensor
        def _(T, ns=ns):
            ctx.wait(T, "T", [ctx.tok("SC"), ctx.tok("V")])
            ctx.inc(T.matmul(g["PSC"][0:1, 0:512], g["c_ones_f"][0:E, 0:1],
                             g["probsf"][0:E, 512 * ns:512 * (ns + 1)], start=True, stop=True), "T")

        @blk.vector
        def _(v, ns=ns):
            ctx.wait(v, "V", ctx.tok("T"))
            ctx.inc(v.reciprocal(out=g["lrowB"][0:1, 512 * ns:512 * (ns + 1)],
                                 in_=g["PSC"][0:1, 0:512]), "V")

    for ns in range(4):
        @blk.tensor
        def _(T, ns=ns):
            ctx.wait(T, "T", ctx.tok("V"))
            ctx.inc(T.matmul(g["PSC"][0:E, 0:512], g["c_onesr_f"][32:33, 0:E],
                             g["lrowB"][0:1, 512 * ns:512 * (ns + 1)], start=True, stop=True), "T")

        @blk.vector
        def _(v, ns=ns):
            ctx.wait(v, "V", ctx.tok("T"))
            sl = slice(512 * ns, 512 * (ns + 1))
            ctx.inc(v.tensor_tensor(out=g["probsf"][0:E, sl], in0=g["probsf"][0:E, sl],
                                    in1=g["PSC"][0:E, 0:512], op=ALU.mult), "V")
    _dump_rows(g, "probs", g["probsf"][0:E, :])

    for t in range(NT):
        @blk.tensor
        def _(T, t=t):
            ctx.wait(T, "T", ctx.tok("V"))
            _li = T.transpose(g["PSC"][0:128, 0:E], g["probsf"][0:E, 128 * t:128 * (t + 1)],
                        g["c_identf"][0:E, 0:E])
            ctx.inc(_li, "T", 1)

        @blk.vector
        def _(v, t=t):
            ctx.wait(v, "V", ctx.tok("T"))
            ctx.inc(v.tensor_copy(out=g["probstm"][:, 8 * t:8 * (t + 1)], in_=g["PSC"][0:128, 0:E]), "V")

    @blk.vector
    def _(v):
        ctx.sw(v, "V")
        ctx.inc(v.tensor_copy(out=g["extr16"][:, :], in_=g["probstm"][:, :]), "V")

    @blk.gpsimd
    def _(gp):
        ctx.wait(gp, "GP", ctx.tok("V"))
        ctx.dma(gp, "GP", out=g["x1s"].rearrange("(t p) r -> p t r", p=128)[:, :, D:D + E],
                             in_=g["extr16"][:, :].rearrange("p (t e) -> p t e", t=NT))

    with g["nc"].named_scope("bisect"):
        _bisect(g)
    with g["nc"].named_scope("indices"):
        _indices(g, 0)


def _ln_tile(g, src_f32, dst16, par=0, vin=None, ow_free=None):
    ctx, blk = g["ctx"], g["blk"]
    b = 8 * par

    @blk.scalar
    def _(sc):
        ctx.wait(sc, "SC", vin if vin else ctx.tok("V"))
        ctx.inc(sc.activation(out=g["lnsq"][:, :], in_=src_f32[:, :], func=AF.Square,
                              accum_out=g["colw"][:, b + 3:b + 4]), "SC")
        ctx.sw(sc, "SC")
        ctx.inc(sc.activation(out=g["lnsq"][:, :], in_=src_f32[:, :], func=AF.Copy,
                              accum_out=g["colw"][:, b + 4:b + 5]), "SC")
    sc1 = ctx.tok("SC")

    @blk.vector
    def _(v):
        ctx.wait(v, "V", sc1)
        cw = g["colw"]
        ctx.inc(v.tensor_scalar(out=cw[:, b + 5:b + 6], in0=cw[:, b + 4:b + 5], scalar1=1.0 / D,
                                scalar2=None, op0=ALU.mult), "V")
        ctx.sw(v, "V")
        ctx.inc(v.tensor_tensor(out=cw[:, b + 6:b + 7], in0=cw[:, b + 5:b + 6],
                                in1=cw[:, b + 5:b + 6], op=ALU.mult), "V")
        ctx.sw(v, "V")
        ctx.inc(v.scalar_tensor_tensor(out=cw[:, b + 6:b + 7], in0=cw[:, b + 3:b + 4], scalar=1.0 / D,
                                       in1=cw[:, b + 6:b + 7], op0=ALU.mult, op1=ALU.subtract), "V")
    v1 = ctx.tok("V")

    @blk.scalar
    def _(sc):
        ctx.wait(sc, "SC", v1)
        ctx.inc(sc.activation(out=g["colw"][:, b + 7:b + 8], in_=g["colw"][:, b + 6:b + 7],
                              func=AF.Sqrt, bias=EPS_LN), "SC")
    sc2 = ctx.tok("SC")

    @blk.vector
    def _(v):
        w = [sc2] if vin else [sc2, ctx.tok("GP"), ctx.tok("SY")]
        if ow_free:
            w.append(ow_free)
        ctx.wait(v, "V", w)
        cw = g["colw"]
        ctx.inc(v.reciprocal(out=cw[:, b + 0:b + 1], in_=cw[:, b + 7:b + 8]), "V")
        ctx.sw(v, "V")
        ctx.inc(v.tensor_tensor(out=cw[:, b + 1:b + 2], in0=cw[:, b + 5:b + 6],
                                in1=cw[:, b + 0:b + 1], op=ALU.mult), "V")
        ctx.sw(v, "V")
        ctx.inc(v.tensor_scalar_mul(cw[:, b + 1:b + 2], cw[:, b + 1:b + 2], -1.0), "V")
        ctx.sw(v, "V")
        ctx.inc(v.tensor_scalar(out=dst16[:, :], in0=src_f32[:, :], scalar1=cw[:, b + 0:b + 1],
                                scalar2=cw[:, b + 1:b + 2], op0=ALU.mult, op1=ALU.add), "V")
    return ctx.tok("V")


def _bisect(g):
    ctx, blk = g["ctx"], g["blk"]
    tw = g["thrw"]

    @blk.vector
    def _(v):
        ctx.inc(v.memset(tw[0:1, 0:8], 0.0), "V")
        ctx.inc(v.memset(tw[0:1, 8:16], 1.0), "V")

    for it in range(BIS_IT + 1):
        last = (it == BIS_IT)

        @blk.vector
        def _(v, last=last):
            ctx.wait(v, "V", ctx.tok("T"))
            if not last:
                ctx.inc(v.tensor_tensor(out=tw[0:1, 16:24], in0=tw[0:1, 0:8], in1=tw[0:1, 8:16],
                                        op=ALU.add), "V")
                ctx.sw(v, "V")
                ctx.inc(v.tensor_scalar_mul(tw[0:1, 16:24], tw[0:1, 16:24], 0.5), "V")
                src = tw[0:1, 16:24]
            else:
                src = tw[0:1, 0:8]
            ctx.sw(v, "V")
            ctx.inc(v.tensor_copy(out=g["mid128"][0:1, :],
                                  in_=src.rearrange("a (b e) -> a b e", b=1).to_broadcast([1, NT, 8])), "V")

        @blk.tensor
        def _(T):
            ctx.wait(T, "T", ctx.tok("V"))
            ctx.inc(T.matmul(g["PS"][5][:, 0:128], g["c_onesr_f"][0:1, :], g["mid128"][0:1, :],
                             start=True, stop=True), "T")

        @blk.vector
        def _(v):
            ctx.wait(v, "V", ctx.tok("T"))
            ctx.inc(v.tensor_tensor(out=g["cmask"][:, :], in0=g["probstm"][:, :],
                                    in1=g["PS"][5][:, 0:128], op=ALU.is_ge), "V")
        if last:
            break

        @blk.tensor
        def _(T):
            ctx.wait(T, "T", ctx.tok("V"))
            ctx.inc(T.matmul(g["PSC"][0:1, 0:128], g["c_ones_h"][:, :], g["cmask"][:, :],
                             start=True, stop=True), "T")

        @blk.vector
        def _(v):
            ctx.wait(v, "V", ctx.tok("T"))
            ctx.inc(v.tensor_reduce(out=tw[0:1, 24:32],
                                    in_=g["PSC"][0:1, 0:128].rearrange("a (t e) -> a e t", e=8),
                                    axis=mybir.AxisListType.X, op=ALU.add), "V")
            ctx.sw(v, "V")
            ctx.inc(v.tensor_scalar(out=g["selw"][0:1, :], in0=tw[0:1, 24:32], scalar1=float(KTOP),
                                    scalar2=None, op0=ALU.is_ge), "V")
            ctx.inc(v.tensor_scalar(out=g["selnw"][0:1, :], in0=tw[0:1, 24:32], scalar1=float(KTOP),
                                    scalar2=None, op0=ALU.is_lt), "V")
            ctx.sw(v, "V")
            ctx.inc(v.copy_predicated(out=tw[0:1, 0:8], mask=g["selw"][0:1, :],
                                      data=tw[0:1, 16:24]), "V")
            ctx.inc(v.copy_predicated(out=tw[0:1, 8:16], mask=g["selnw"][0:1, :],
                                      data=tw[0:1, 16:24]), "V")
    @blk.vector
    def _(v):
        ctx.sw(v, "V")
        ctx.inc(v.tensor_copy(out=g["cmaski"][:, :], in_=g["cmask"][:, :]), "V")
    _dump_rows(g, "thr", tw[0:1, 0:8])


def _indices(g, e):
    ctx, blk = g["ctx"], g["blk"]
    for e in (e,):
        @blk.vector
        def _(v, e=e):
            ctx.wait(v, "V", ctx.tok("T"))
            ctx.inc(v.tensor_copy(out=g["cand"][:, :], in_=g["c_neg1"][:, :]), "V")
            ctx.sw(v, "V")
            ctx.inc(v.copy_predicated(out=g["cand"][:, :], mask=g["cmaski"][:, e:e + 121:8],
                                      data=g["c_iota_tok"][:, :]), "V")
            ctx.sw(v, "V")
            ctx.inc(v.max(out=g["s16"][:, 0:8], in_=g["cand"][:, :]), "V")
            ctx.sw(v, "V")
            ctx.inc(v.match_replace(out=g["cand"][:, :], in_to_replace=g["s16"][:, 0:8],
                                    in_values=g["cand"][:, :], imm_value=-1.0), "V")
            ctx.sw(v, "V")
            ctx.inc(v.max(out=g["s16"][:, 8:16], in_=g["cand"][:, :]), "V")
            ctx.sw(v, "V")
            ctx.inc(v.tensor_scalar(out=g["s16f"][:, :], in0=g["s16"][:, :], scalar1=0.0,
                                    scalar2=None, op0=ALU.max), "V")
            ctx.inc(v.tensor_reduce(out=g["kcol"][:, 0:1], in_=g["cmask"][:, e:e + 121:8],
                                    axis=mybir.AxisListType.X, op=ALU.add), "V")

        @blk.tensor
        def _(T, e=e):
            ctx.wait(T, "T", ctx.tok("V"))
            ctx.inc(T.matmul(g["PSC"][:, 0:1], g["c_maskSU"][:, :], g["kcol"][:, 0:1],
                             start=True, stop=True), "T")

        @blk.vector
        def _(v, e=e):
            ctx.wait(v, "V", ctx.tok("T"))
            ctx.inc(v.tensor_copy(out=g["basec"][:, 0:1], in_=g["PSC"][:, 0:1]), "V")
            for s in range(16):
                ctx.sw(v, "V")
                ctx.inc(v.tensor_scalar(out=g["basec"][:, 1:2], in0=g["basec"][:, 0:1],
                                        scalar1=float(s), scalar2=None, op0=ALU.add), "V")
                ctx.sw(v, "V")
                ctx.inc(v.tensor_scalar(out=g["eqw"][:, :], in0=g["c_iota_d"][:, :],
                                        scalar1=g["basec"][:, 1:2], scalar2=None,
                                        op0=ALU.is_equal), "V")
                ctx.sw(v, "V")
                if s == 0:
                    ctx.inc(v.tensor_scalar(out=g["Mplace"][:, :], in0=g["eqw"][:, :],
                                            scalar1=g["s16f"][:, 0:1], scalar2=None, op0=ALU.mult), "V")
                else:
                    ctx.inc(v.scalar_tensor_tensor(out=g["Mplace"][:, :], in0=g["eqw"][:, :],
                                                   scalar=g["s16f"][:, s:s + 1], in1=g["Mplace"][:, :],
                                                   op0=ALU.mult, op1=ALU.add), "V")

        @blk.tensor
        def _(T, e=e):
            ctx.wait(T, "T", ctx.tok("V"))
            ctx.inc(T.matmul(g["PSC"][0:1, 0:KTOP], g["c_ones_h"][:, :], g["Mplace"][:, :],
                             start=True, stop=True), "T")

        @blk.vector
        def _(v, e=e):
            ctx.wait(v, "V", ctx.tok("T"))
            ctx.inc(v.tensor_copy(out=g["idxrow"][0:1, :], in_=g["PSC"][0:1, 0:KTOP]), "V")

        @blk.tensor
        def _(T, e=e):
            ctx.wait(T, "T", ctx.tok("V"))
            for c in range(4):
                _li = T.transpose(g["PSC"][0:128, 128 + c:129 + c], g["idxrow"][0:1, 128 * c:128 * (c + 1)],
                            g["c_identf"][0:1, 0:1])
            ctx.inc(_li, "T", 1)

        @blk.vector
        def _(v, e=e):
            ctx.wait(v, "V", ctx.tok("T"))
            ctx.inc(v.tensor_copy(out=g["idxc"][:, 4 * e:4 * e + 4], in_=g["PSC"][0:128, 128:132]), "V")

    g.setdefault("idx_tok", {})[e] = ctx.tok("V")

    if e == E - 1 and "idx" in g["DBG"]:
        @blk.vector
        def _(v):
            ctx.wait(v, "V", ctx.barrier_toks())
            ctx.inc(v.tensor_copy(out=g["probstm"][:, 0:32], in_=g["idxc"][:, :]), "V")
        _dump_rows(g, "idx", g["probstm"][:, 0:32])


def _phase4(g):
    ctx, blk = g["ctx"], g["blk"]
    PS = g["PS"]
    ga_tok = [None] * (E + 1)
    tr_tok = [None] * (E + 1)
    mm1_done = [None] * (E + 1)
    sc1_done = {}
    mm_t = {}
    w1g_tok = {}
    w2_tok = {}
    scat_t = [None] * E

    txp_v = {}
    txp_done = {}
    bank_last = {0: None}

    def emit_route(e):
        par = e % 2

        @blk.gpsimd
        def _(gp, e=e, par=par):
            if e == 0:
                ctx.wait(gp, "GP", [ctx.tok("V"), ctx.tok("SY"), ctx.tok("T")])
                gp.wait_ge(g["wc_sem"], 16 * g["n_wcast"])
            tk = txp_done.get(e - 2)
            if tk:
                ctx.wait(gp, "GP", tk)
            ctx.wait(gp, "GP", g["idx_tok"][e])
            for c in range(4):
                ctx.dma(gp, "GP", indirect=True,
                    out=g["gtmP"][par][:, XR * c:XR * (c + 1)],
                    out_offset=None,
                    in_=g["x1s"][:, :],
                    in_offset=IndirectOffsetOnAxis(g["idxc"][:, 4 * e + c:4 * e + c + 1], 0))
        ga_tok[e] = ctx.tok("GP")

    def emit_txp(e, r):
        par = e % 2
        bank = g["PSB"]

        @blk.tensor
        def _(T, e=e, r=r, par=par, bank=bank):
            w = [ga_tok[e]]
            tk = bank_last[0]
            if tk:
                w.append(tk)
            if e == 0 and r == 0:
                w.append(ctx.tok("V"))
            ctx.wait(T, "T", w)
            for c in range(4):
                _li = T.transpose(bank[:, 128 * c:128 * (c + 1)],
                                  g["gtmP"][par][:, XR * c + 128 * r:XR * c + 128 * (r + 1)],
                                  g["c_identb"][:, :])
            ctx.inc(_li, "T", 1)
        tt = ctx.tok("T")
        if r == 8:
            txp_done[e] = tt

        @blk.vector
        def _(v, e=e, r=r, par=par, bank=bank, tt=tt):
            ctx.wait(v, "V", tt)
            ctx.inc(v.tensor_copy(out=g["xe_fmP"][par][:, KTOP * r:KTOP * (r + 1)],
                                  in_=bank[:, 0:512]), "V")
        txp_v[(e, r)] = bank_last[0] = ctx.tok("V")

    def emit_gcol(e):
        par = e % 2

        @blk.sync
        def _(sy, e=e, par=par):
            ctx.wait(sy, "SY", txp_v[(e, 8)])
            ctx.dma(sy, "SY", out=g["growP"][par][0:1, :],
                                 in_=g["xe_fmP"][par][e:e + 1, KTOP * 8:KTOP * 9])
        grow_t = ctx.tok("SY")

        @blk.tensor
        def _(T, e=e, par=par, grow_t=grow_t):
            w = [grow_t]
            if bank_last[0]:
                w.append(bank_last[0])
            ctx.wait(T, "T", w)
            for c in range(4):
                _li = T.transpose(g["PSB"][0:128, 2 * c:2 * c + 1],
                                  g["growP"][par][0:1, 128 * c:128 * (c + 1)],
                                  g["c_identb"][0:1, 0:1])
            ctx.inc(_li, "T", 1)
        gc_t = ctx.tok("T")

        @blk.vector
        def _(v, e=e, gc_t=gc_t):
            ctx.wait(v, "V", gc_t)
            ctx.inc(v.tensor_copy(out=g["gcol"][:, 4 * e:4 * e + 4], in_=g["PSB"][0:128, 0:7:2]), "V")
        bank_last[0] = ctx.tok("V")

    def emit_w1g(e, grp, prev_tok):
        @blk.sync
        def _(sy, e=e, grp=grp, prev_tok=prev_tok):
            if prev_tok:
                ctx.wait(sy, "SY", prev_tok)
            for mm in range(4 * grp, 4 * grp + 4):
                ctx.dma(sy, "SY",
                    out=g["w1s2"][mm % 8][:, :].rearrange("p (kk q) -> p kk q", kk=8),
                    in_=g["w116"][e][:, 128 * mm:128 * (mm + 1)].rearrange("(kk p) q -> p kk q", p=128))
        w1g_tok[(e, grp)] = ctx.tok("SY")

    def emit_w2(e):
        @blk.sync
        def _(sy, e=e):
            tk = mm_t.get((e - 1, "mm2"))
            if tk:
                ctx.wait(sy, "SY", tk)
            for kk2 in range(16):
                ctx.dma(sy, "SY", out=g["w2s"][kk2][:, :],
                                     in_=g["w216"][e][128 * kk2:128 * (kk2 + 1), :])
        w2_tok[e] = ctx.tok("SY")

    emit_route(0)
    for r in range(9):
        emit_txp(0, r)
    emit_gcol(0)
    emit_w1g(0, 0, None)
    emit_w1g(0, 1, None)

    for e in range(E):
        par = e % 2
        for m in range(16):
            if m == 0 and e + 1 < E:
                _indices(g, e + 1)
                emit_route(e + 1)
            if m == 2:
                emit_w2(e)
            if m >= 6 and m % 2 == 0 and e + 1 < E:
                emit_txp(e + 1, (m - 6) // 2)
            @blk.tensor
            def _(T, e=e, m=m, par=par):
                ctx.wait(T, "T", [txp_v[(e, 7)], w1g_tok[(e, m // 4)]])
                tk = sc1_done.get((e, m - 4))
                if tk:
                    ctx.wait(T, "T", tk)
                elif m < 4:
                    tk2 = mm_t.get((e - 1, "mm2sc"))
                    if tk2:
                        ctx.wait(T, "T", tk2)
                pb = PS[m % 4]
                for kk in range(8):
                    _li = T.matmul(pb[:, 0:512],
                             g["w1s2"][m % 8][:, :].rearrange("p (kk q) -> p kk q", kk=8)[:, kk, :],
                             g["xe_fmP"][par][:, KTOP * kk:KTOP * (kk + 1)], start=(kk == 0), stop=(kk == 7))
                ctx.inc(_li, "T", 1)
            mm_t[(e, m)] = ctx.tok("T")
            if m % 4 == 3:
                gnext = m // 4 + 2
                if gnext < 4:
                    emit_w1g(e, gnext, mm_t[(e, m)])
                elif e + 1 < E:
                    emit_w1g(e + 1, gnext - 4, mm_t[(e, m)])

            @blk.scalar
            def _(sc, e=e, m=m):
                ctx.wait(sc, "SC", mm_t[(e, m)])
                ctx.inc(sc.activation(out=g["h_fm"][m], in_=PS[m % 4][:, 0:512], func=AF.Silu), "SC")
            sc1_done[(e, m)] = ctx.tok("SC")
        mm1_done[e] = mm_t[(e, 15)]

        y_sc = {}
        mmt = None
        for tm_ in range(4):
            if e + 1 < E:
                emit_txp(e + 1, 5 + tm_)
                if tm_ == 3:
                    emit_gcol(e + 1)
            @blk.tensor
            def _(T, e=e, tm_=tm_):
                ctx.wait(T, "T", [sc1_done[(e, 15)], w2_tok[e]])
                tk = y_sc.get(tm_ - 1)
                if tk:
                    ctx.wait(T, "T", tk)
                elif tm_ == 0:
                    tk2 = mm_t.get((e - 1, "mm2sc"))
                    if tk2:
                        ctx.wait(T, "T", tk2)
                    elif e == 0:
                        ctx.wait(T, "T", ctx.tok("V"))
                for kk2 in range(16):
                    for nh in range(2):
                        _li = T.matmul(PS[4 + nh][:, 0:512],
                                 g["h_fm"][kk2][:, 128 * tm_:128 * (tm_ + 1)],
                                 g["w2s"][kk2][:, 512 * nh:512 * (nh + 1)],
                                 start=(kk2 == 0), stop=(kk2 == 15))
                ctx.inc(_li, "T", 1)
            mmt = ctx.tok("T")

            @blk.vector
            def _(v, e=e, tm_=tm_, mmt=mmt):
                ctx.wait(v, "V", mmt)
                if e > 0 and scat_t[e - 1]:
                    ctx.wait(v, "V", scat_t[e - 1])
                for nh in range(2):
                    ctx.inc(v.tensor_scalar(
                        out=g["yscat"][:, 1024 * tm_ + 512 * nh:1024 * tm_ + 512 * (nh + 1)],
                        in0=PS[4 + nh][:, 0:512],
                        scalar1=g["gcol"][:, 4 * e + tm_:4 * e + tm_ + 1],
                        scalar2=None, op0=ALU.mult), "V")
            y_sc[tm_] = ctx.tok("V")
        mm_t[(e, "mm2")] = mmt
        mm_t[(e, "mm2sc")] = y_sc[3]

        @blk.gpsimd
        def _(gp, e=e, y3=y_sc[3]):
            ctx.wait(gp, "GP", y3)
            for c in range(4):
                ctx.dma(gp, "GP", indirect=True, 
                    out=g["accd"][:, :],
                    out_offset=IndirectOffsetOnAxis(g["idxc"][:, 4 * e + c:4 * e + c + 1], 0),
                    in_=g["yscat"][:, 1024 * c:1024 * (c + 1)], in_offset=None,
                    compute_op=ALU.add)
        scat_t[e] = ctx.tok("GP")
    g["scat_tok"] = ctx.tok("GP")

    if "acc" in g["DBG"]:
        for t in range(NT):
            @blk.sync
            def _(sy, t=t):
                ctx.wait(sy, "SY", ctx.barrier_toks())
                ctx.dma(sy, "SY", out=g["DBG"]["acc"][128 * t:128 * (t + 1), :],
                                     in_=g["accd"][128 * t:128 * (t + 1), :])


def _phase5(g):
    ctx, blk = g["ctx"], g["blk"]
    fin_v = {}
    add_v = {}
    out_sy = {}
    for t in range(NT):
        par = t % 2

        @blk.sync
        def _(sy, t=t, par=par):
            ctx.wait(sy, "SY", g["scat_tok"])
            if t >= 2:
                ctx.wait(sy, "SY", fin_v[t - 2])
            ctx.dma(sy, "SY", out=g["resw"][par][:, :], in_=g["accd"][128 * t:128 * (t + 1), :])
        acc_sy = ctx.tok("SY")

        @blk.gpsimd
        def _(gp, t=t, par=par):
            if t >= 2:
                ctx.wait(gp, "GP", add_v[t - 2])
            ctx.dma(gp, "GP", out=g["x1stP"][par][:, :], in_=g["x1s"][128 * t:128 * (t + 1), 0:D])
        x1_gp = ctx.tok("GP")

        @blk.vector
        def _(v, t=t, par=par, acc_sy=acc_sy, x1_gp=x1_gp):
            ctx.wait(v, "V", [acc_sy, x1_gp])
            ctx.inc(v.tensor_tensor(out=g["resw"][par][:, :], in0=g["resw"][par][:, :],
                                    in1=g["x1stP"][par][:, :], op=ALU.add), "V")
        add_v[t] = ctx.tok("V")
        fin_v[t] = _ln_tile(g, g["resw"][par], g["outwP"][par], par=par,
                            vin=add_v[t], ow_free=out_sy.get(t - 2))

        @blk.sync
        def _(sy, t=t, par=par):
            ctx.wait(sy, "SY", fin_v[t])
            ctx.dma(sy, "SY", out=g["out"][128 * t:128 * (t + 1), :], in_=g["outwP"][par][:, :])
        out_sy[t] = ctx.tok("SY")

    @blk.sync
    def _(sy):
        sy.wait_ge(ctx.sem["SY"], ctx.cnt["SY"])


def host_in_map(inputs, core):
    f32 = np.float32
    hs = np.ascontiguousarray(np.asarray(inputs["L0_attn_hidden_states"], f32)[core])
    wq = np.asarray(inputs["L0_attn_Wq"], f32)
    wk = np.asarray(inputs["L0_attn_Wk"], f32)
    wv = np.asarray(inputs["L0_attn_Wv"], f32)
    wb = np.asarray(inputs["L0_attn_Wb"], f32)
    wdec = np.asarray(inputs["L0_attn_Wdec"], f32)
    wmix = np.asarray(inputs["L0_attn_Wmix"], f32)
    ws = np.zeros((16, D), f32)
    ws[0:4] = wb
    ws[4:8] = wdec
    ws[8:12] = wmix
    convw = np.concatenate([np.asarray(inputs["L0_attn_wq_conv"], f32),
                            np.asarray(inputs["L0_attn_wk_conv"], f32),
                            np.asarray(inputs["L0_attn_wv_conv"], f32)], axis=1)
    return {
        "hs": hs,
        "wqT": np.ascontiguousarray(wq.T),
        "wkT": np.ascontiguousarray(wk.T),
        "wvT": np.ascontiguousarray(wv.T),
        "wsT": np.ascontiguousarray(ws.T),
        "mixb": np.asarray(inputs["L0_attn_mix_bias"], f32).reshape(1, 4),
        "convw": np.ascontiguousarray(convw),
        "onw": np.asarray(inputs["L0_attn_onorm_w"], f32).reshape(1, DK),
        "woT": np.ascontiguousarray(np.asarray(inputs["L0_attn_Wo"], f32).T),
        "cwT": np.ascontiguousarray(np.asarray(inputs["L0_ffn_choice_w"], f32).T),
        "w1": np.ascontiguousarray(np.asarray(inputs["L0_ffn_w1"], f32)),
        "w2": np.ascontiguousarray(np.asarray(inputs["L0_ffn_w2"], f32)),
    }


_BUILD_CACHE = {}


def kernel(_dbg=(), _upto="p5", **inputs):
    key = (tuple(sorted(_dbg)), _upto)
    if key not in _BUILD_CACHE:
        _BUILD_CACHE[key] = build(_dbg, _upto)
    nc, DBG = _BUILD_CACHE[key]
    in_maps = [host_in_map(inputs, core) for core in range(8)]
    res = run_bass_kernel_spmd(nc, in_maps, core_ids=list(range(8)))
    out = np.stack([r["out"] for r in res.results], axis=0)
    dbgout = {k: [r["dbg_" + k] for r in res.results] for k in DBG}
    return (out, dbgout) if (_dbg or _upto != "p5") else out
